# version 08
# baseline (speedup 1.0000x reference)
import numpy as np
import ml_dtypes
import concourse.bass as bass
import concourse.mybir as mybir
from concourse.bass_utils import run_bass_kernel_spmd

F32 = mybir.dt.float32
F16 = mybir.dt.float16
ALU = mybir.AluOpType
ACT = mybir.ActivationFunctionType
AX = mybir.AxisListType.X

T, NT, H, HC, I, IC = 4096, 32, 1024, 8, 2816, 22
WIN = 128
EPS = 1e-5
RC = 12582912.0
CHI, CLO = 127.25, -128.25
BNEG = -1e30
GRP, GW = 8, 4
RGRP = [[0, 1, 2, 3, 4, 5, 6, 7]]


class Emit:
    ENGS = ("sync", "scalar", "vector", "tensor", "gpsimd")

    def __init__(self, nc):
        self.nc = nc
        self.q = {e: [] for e in self.ENGS}
        self.sems = {}
        self.cnt = {}
        self._ctx = []

    def sem(self, name):
        h = self.nc.semaphore(name)
        s = h.__enter__()
        self._ctx.append(h)
        self.sems[name] = s
        self.cnt[name] = 0

    def op(self, eng, fn, inc=None, n=1):
        if inc is None:
            self.q[eng].append(fn)
        else:
            s = self.sems[inc]
            self.cnt[inc] += n
            self.q[eng].append(lambda e, fn=fn, s=s, n=n: fn(e).then_inc(s, n))

    def wait(self, eng, semname, val=None):
        v = self.cnt[semname] if val is None else val
        assert v <= self.cnt[semname], f"wait beyond appended count on {semname}: {v} > {self.cnt[semname]}"
        s = self.sems[semname]
        if v > 0:
            self.q[eng].append(lambda e, s=s, v=v: e.wait_ge(s, v))

    def val(self, semname):
        return self.cnt[semname]

    def replay(self, block):
        q = self.q

        @block.sync
        def _(e):
            for f in q["sync"]:
                f(e)

        @block.scalar
        def _(e):
            for f in q["scalar"]:
                f(e)

        @block.vector
        def _(e):
            for f in q["vector"]:
                f(e)

        @block.tensor
        def _(e):
            for f in q["tensor"]:
                f(e)

        @block.gpsimd
        def _(e):
            for f in q["gpsimd"]:
                f(e)


def build(dbg=()):
    dbg = set(dbg)
    nc = bass.Bass("TRN2", target_bir_lowering=False, debug=False)
    ctx = []

    def keep(h):
        t = h.__enter__()
        ctx.append(h)
        return t

    x_d = nc.dram_tensor("x", [T, H], F32, kind="ExternalInput").ap()
    wq_d = nc.dram_tensor("wq", [H, H], F32, kind="ExternalInput").ap()
    wk_d = nc.dram_tensor("wk", [H, H], F32, kind="ExternalInput").ap()
    wv_d = nc.dram_tensor("wv", [H, H], F32, kind="ExternalInput").ap()
    bq_d = nc.dram_tensor("bq", [1, H], F32, kind="ExternalInput").ap()
    bk_d = nc.dram_tensor("bk", [1, H], F32, kind="ExternalInput").ap()
    bv_d = nc.dram_tensor("bv", [1, H], F32, kind="ExternalInput").ap()
    gam_d = nc.dram_tensor("gam", [1, H], F32, kind="ExternalInput").ap()
    bet_d = nc.dram_tensor("bet", [1, H], F32, kind="ExternalInput").ap()
    clip_d = nc.dram_tensor("clip", [1, 2], F32, kind="ExternalInput").ap()
    wg_d = nc.dram_tensor("wg", [I, H], F32, kind="ExternalInput").ap()
    wu_d = nc.dram_tensor("wu", [I, H], F32, kind="ExternalInput").ap()
    wd_d = nc.dram_tensor("wd", [H, I], F32, kind="ExternalInput").ap()
    id_d = nc.dram_tensor("ident", [128, 128], F16, kind="ExternalInput").ap()
    out_d = nc.dram_tensor("out", [T, H], F32, kind="ExternalOutput").ap()

    def scr(name, shape, dt):
        kind = "ExternalOutput" if name in dbg else "Internal"
        return nc.dram_tensor(name, shape, dt, kind=kind).ap()

    xnf_d = scr("xnf", [T, H], F16)
    g_d = scr("gdbg", [T, H], F16) if "gdbg" in dbg else None
    tsp_d = scr("tsp", [IC, 128, T], F32)
    ssp_d = scr("ssp", [T, H], F16)
    scal_d = scr("scal", [6, 128], F32)
    cc1i = nc.dram_tensor("cc1i", [1, 16], F32, kind="Internal").ap()
    cc1o = nc.dram_tensor("cc1o", [1, 16], F32, kind="Internal").ap()
    cc2i = nc.dram_tensor("cc2i", [1, 16], F32, kind="Internal").ap()
    cc2o = nc.dram_tensor("cc2o", [1, 16], F32, kind="Internal").ap()

    sb = lambda n, s, d, **k: keep(nc.sbuf_tensor(n, s, d, **k))

    Bar = sb("arB", [128, 32768], F16, is_transpose_dest=True)
    Car = sb("arC", [128, 32768], F16, is_transpose_dest=True)
    Dar = sb("arD", [128, 30720], F16, is_transpose_dest=True)
    Bf, Cf, Df = Bar.ap(), Car.ap(), Dar.ap()
    Bf32, Cf32, Df32 = Bf.bitcast(F32), Cf.bitcast(F32), Df.bitcast(F32)

    xr = Bf32[:, 0:6144].rearrange("p (r k) -> p r k", r=6)
    xnt = Bf[:, 12288:16384].rearrange("p (r k) -> p r k", r=4)
    nf = Bf.rearrange("p (c t) -> p c t", c=HC)
    mdT = Bf[:, 0:22528].rearrange("p (c i k) -> p c i k", c=HC, i=IC)

    qkvT = Cf[:, 0:24576].rearrange("p (m o) -> p m o", m=24)
    xng = Cf[:, 24576:28672].rearrange("p (c t) -> p c t", c=HC)
    vg = Cf[:, 28672:32768].rearrange("p (w k) -> p w k", w=GW)

    def lhsr_ap(ic, gu):
        base = 24576 + (ic % 4) * 2048 + gu * 1024
        return Cf[:, base:base + 1024].rearrange("p (c k) -> p c k", c=HC)

    hqv = Cf[:, 0:22528].rearrange("p (r i k) -> p r i k", r=2, i=IC)

    wstr = Df32[:, 0:2048].rearrange("p (r k) -> p r k", r=2)
    x3r = Df32[:, 0:2048].rearrange("p (r k) -> p r k", r=2)
    x2r = Df32[:, 0:2048].rearrange("p (r k) -> p r k", r=2)
    mnat = Df[:, 4096:6144].rearrange("p (r k) -> p r k", r=2)
    qg = Bf[:, 16384:24576].rearrange("p (r c t) -> p r c t", r=2, c=8)
    kg = Bf[:, 24576:32768].rearrange("p (r c t) -> p r c t", r=2, c=8)
    sall = Df32[:, 3072:3584].rearrange("p (w k) -> p w k", w=GW)
    pall = Df32[:, 3584:4096].rearrange("p (w k) -> p w k", w=GW)
    scrx = Df32[:, 4096:4608].rearrange("p (w k) -> p w k", w=GW)
    t32 = Df32[:, 4608:4736].rearrange("p (w k) -> p w k", w=GW)
    et32 = Df32[:, 4736:4864].rearrange("p (w k) -> p w k", w=GW)
    zsv = Df32[:, 4864:4880].rearrange("p (w k) -> p w k", w=GW)
    attn16 = Df[:, 9760:10784].rearrange("p (r w k) -> p r w k", r=2, w=GW)
    attnT = Df[:, 10784:11296].rearrange("p (w k) -> p w k", w=GW)
    xne = Df[:, 18976:21024].rearrange("p (r k) -> p r k", r=2)
    rrb = Df[:, 21024:25120].rearrange("p (r k) -> p r k", r=4)
    snb = Df[:, 25120:26144]
    stile = Df[:, 26144:28192].rearrange("p (r k) -> p r k", r=2)
    rlsc = Df32[:, 14096:15120]
    p2sc = Df32[:, 2048:4096].rearrange("p (r k) -> p r k", r=2)
    xnt2 = Df[:, 8192:10240].rearrange("p (r k) -> p r k", r=2)
    wstrF = Cf32[:, 0:2048].rearrange("p (r k) -> p r k", r=2)
    wsc = Df32[:, 7168:8192]
    mnat1 = Df[:, 6144:8192].rearrange("p (r k) -> p r k", r=2)
    slu = Df32[:, 6144:6656]
    slu2 = Df32[:, 6656:7168]
    trng = Df32[:, 5120:6144].rearrange("p (r k) -> p r k", r=2)
    wdh = Df32[:, 0:1408]
    wdsc = Df32[:, 1408:2816]
    md16 = Df[:, 5632:5632 + I]
    tld = Df32[:, 0:5632].rearrange("p (i k) -> p i k", i=IC)
    gfm = Df[:, 11264:15360].rearrange("p (c k) -> p c k", c=HC)
    gtmq = Df[:, 15360:23552].rearrange("p (r w k) -> p r w k", r=2, w=GW)
    xr4 = Df32[:, 11776:13824].rearrange("p (r k) -> p r k", r=2)
    sld = Bf[:, 26624:28672].rearrange("p (r k) -> p r k", r=2)
    obuf = Bf32[:, 11264:13312].rearrange("p (r k) -> p r k", r=2)

    mv = sb("mv", [128, NT, 2], F32)
    rho = sb("rho", [128, NT], F32)
    negmu = sb("negmu", [128, NT], F32)
    nmr = sb("nmr", [128, NT], F32)
    amax = sb("amax", [128, NT], F32)
    rmxc = sb("rmxc", [128, NT], F32)
    rmnc = sb("rmnc", [128, NT], F32)
    Aqv = sb("Aqv", [128, NT], F32)
    Bqv = sb("Bqv", [128, NT], F32)
    nwt = sb("nwt", [128, 4, 4], F32)
    biasq = sb("biasq", [128, 8, 3], F32)
    gamr = sb("gamr", [128, H], F16)
    betr = sb("betr", [128, H], F16)
    srow = sb("srow", [128, IC, 2], F32)
    cgu = sb("cgu", [128, IC, 2], F32)
    srd = sb("srd", [128, HC], F32)
    cdw = sb("cdw", [128, HC], F32)
    gmc = sb("gmc", [128, IC * 8], F32)
    wr127 = sb("wr127", [128, 2, 4], F32)
    sc1 = sb("sc1", [1, 16], F32)
    scb = sb("scb", [128, 4], F32)
    foldb = sb("foldb", [1, 128], F32)
    lmax = sb("lmax", [128, 2], F32)
    ccb = sb("ccb", [1, 16], F32)
    clipsb = sb("clipsb", [1, 2], F32)
    c32 = sb("c32", [128, 1], F32)
    onef = sb("onef", [1, 128], F16)
    idsb = sb("idsb", [128, 128], F16)
    bv16 = sb("bv16", [1, H], F16)
    bvf = sb("bvf", [1, H], F32)
    mv2 = sb("mv2", [128, GW, 2], F32)
    bnb = sb("bnb", [128, 2, 12], F32)
    rho2 = sb("rho2", [128, GW], F32)
    nmr2 = sb("nmr2", [128, GW], F32)
    nwt2 = sb("nwt2", [128, GW, 2], F32)
    jnk = sb("jnk", [128, 64], F32)

    psb = [keep(nc.psum_tensor(f"pbank{i}", [128, 512], F32)) for i in range(8)]
    pb = [p.ap() for p in psb]
    pb16 = [p.bitcast(F16) for p in pb]

    E = Emit(nc)
    for s in ("x1 xrdv veps rho xn xf "
              "qld qt sig "
              "gm1 f1 ar1 vec1 "
              "x2 v2 ntm nfm "
              "wst wsg wq1 "
              "gmm evg umm stt sp gmr gm2 f2 ar2 vec2 "
              "xgl qmm qev vmm vev smm sev ext esn attn atmm atv avmm "
              "x3 xne r lrl rh2 sq2 sn st ssp "
              "wdl wdq wdt "
              "tld hq dmm evd gt x4 sldm sg ost dbg").split():
        E.sem("s_" + s)
    OP, W, V = E.op, E.wait, E.val

    def snop(n=2):
        for _ in range(n):
            OP("scalar", lambda e: e.activation(jnk[:, 0:64], jnk[:, 0:64], ACT.Identity))

    def vnop(n=3):
        for _ in range(n):
            OP("vector", lambda e: e.engine_nop())

    OP("scalar", lambda e: e.dma_start(out=clipsb[:], in_=clip_d), "s_qld", 16)
    OP("scalar", lambda e: e.dma_start(
        out=biasq[:, :, 0], in_=bq_d.rearrange("z (c p) -> p (z c)", p=128),
        allow_slow_non_contiguous=True), "s_qld", 16)
    OP("scalar", lambda e: e.dma_start(
        out=biasq[:, :, 1], in_=bk_d.rearrange("z (c p) -> p (z c)", p=128),
        allow_slow_non_contiguous=True), "s_qld", 16)
    OP("scalar", lambda e: e.dma_start(out=idsb[:], in_=id_d), "s_qld", 16)
    OP("scalar", lambda e: e.dma_start(out=bvf[:], in_=bv_d), "s_qld", 16)
    OP("gpsimd", lambda e: e.dma_start(out=gamr[:], in_=gam_d.to_broadcast([128, H])), "s_qld", 16)
    OP("gpsimd", lambda e: e.dma_start(out=betr[:], in_=bet_d.to_broadcast([128, H])), "s_qld", 16)
    qld_all = V("s_qld")
    W("vector", "s_qld", qld_all)
    OP("vector", lambda e: e.memset(jnk[:], 1.0))
    OP("vector", lambda e: e.memset(c32[:], 1.0 / 32.0))
    OP("vector", lambda e: e.memset(onef[:], 1.0))
    OP("vector", lambda e: e.tensor_copy(bv16[:], bvf[:]))
    OP("vector", lambda e: e.tensor_scalar(biasq[:, :, 2], biasq[:, :, 0], 1.0 / 32.0, None, op0=ALU.mult), "s_vec1", 1)
    W("scalar", "s_qld", qld_all)
    snop(1)
    OP("scalar", lambda e: e.activation(sc1[0:1, 2:4], clipsb[0:1, 0:2], ACT.Exp, scale=-1.0), "s_sig", 1)
    W("vector", "s_sig", 1)
    OP("vector", lambda e: e.tensor_scalar(sc1[0:1, 2:4], sc1[0:1, 2:4], 1.0, None, op0=ALU.add))
    vnop()
    OP("vector", lambda e: e.reciprocal(sc1[0:1, 4:6], sc1[0:1, 2:4]), "s_sig", 1)

    qstripe_qt = {}
    for k24 in range(24):
        mi, st8 = divmod(k24, 8)
        wsrc = (wq_d, wk_d, wv_d)[mi]
        ldq = "scalar" if k24 % 2 == 0 else "gpsimd"
        ldsem = "s_wst" if k24 % 2 == 0 else "s_wsg"
        if k24 >= 2:
            W(ldq, "s_wq1", k24 - 1)
        OP(ldq, lambda e, wsrc=wsrc, st8=st8, k24=k24: e.dma_start(
            out=wstr[:, k24 % 2, :], in_=wsrc[st8 * 128:(st8 + 1) * 128, :]), ldsem, 16)
        W("vector", ldsem, V(ldsem))
        if k24 >= 2:
            W("vector", "s_qt", qstripe_qt[k24 - 2])
        OP("vector", lambda e, k24=k24: e.tensor_copy(mnat[:, k24 % 2, :], wstr[:, k24 % 2, :]), "s_wq1", 1)
        W("scalar", "s_wq1", V("s_wq1"))
        OP("scalar", lambda e, mi=mi, st8=st8, k24=k24: e.dma_start(
            out=qkvT[:, mi * 8:(mi + 1) * 8, st8 * 128:(st8 + 1) * 128],
            in_=mnat[:, k24 % 2, :].rearrange("p (c k) -> p c k", c=HC),
            transpose=True), "s_qt", 16)
        qstripe_qt[k24] = V("s_qt")
    qkvT_done = V("s_qt")

    vmm_hist, lrl_hist, r_hist, xne_hist, qev_hist = [], [], [], [], []
    smm_at, attn_at, av_at, sev_at, atmm_at = {}, {}, {}, {}, {}

    def emit_A(g):
        if g == 0:
            W("tensor", "s_qt", qkvT_done)
        if g >= 1:
            W("sync", "s_vmm", vmm_hist[-1])
        W("sync", "s_xf", 16 * (g * GW + GW))
        for w in range(GW):
            j4 = g * GW + w
            OP("sync", lambda e, j4=j4, w=w: e.dma_start(
                out=xng[:, :, w * 128:(w + 1) * 128],
                in_=xnf_d[j4 * 128:(j4 + 1) * 128, :].rearrange("t (c k) -> t c k", c=HC),
                transpose=True), "s_xgl", 16)
        W("tensor", "s_xgl", V("s_xgl"))

        for qk in range(2):
            for oc in range(HC):
                bank = oc % 2
                if len(qev_hist) >= 2:
                    W("tensor", "s_qev", qev_hist[-2])
                for hc in range(HC):
                    OP("tensor", lambda e, bank=bank, qk=qk, hc=hc, oc=oc, g=g: e.matmul(
                        pb[bank], qkvT[:, qk * 8 + hc, oc * 128:(oc + 1) * 128], xng[:, hc, :],
                        start=(hc == 0), stop=(hc == HC - 1)),
                        "s_qmm" if hc == HC - 1 else None, 1)
                W("scalar", "s_qmm", V("s_qmm"))
                if g >= 2:
                    W("scalar", "s_smm", smm_at[g - 2])
                if qk == 0:
                    OP("scalar", lambda e, oc=oc, bank=bank, g=g: e.activation(
                        qg[:, g % 2, oc, :], pb[bank], ACT.Identity, scale=c32[:, 0:1],
                        bias=biasq[:, oc, 2:3]), "s_qev", 1)
                else:
                    OP("scalar", lambda e, oc=oc, bank=bank, g=g: e.activation(
                        kg[:, g % 2, oc, :], pb[bank], ACT.Identity, bias=biasq[:, oc, 1:2]), "s_qev", 1)
                qev_hist.append(V("s_qev"))

        W("tensor", "s_qev", V("s_qev"))
        W("tensor", "s_sev", V("s_sev"))
        ps_sc = pb[4].rearrange("p (w k) -> p w k", w=GW)
        for w in range(GW):
            for oc in range(HC):
                OP("tensor", lambda e, w=w, oc=oc, ps_sc=ps_sc, g=g: e.matmul(
                    ps_sc[:, w, :], qg[:, g % 2, oc, w * 128:(w + 1) * 128],
                    kg[:, g % 2, oc, w * 128:(w + 1) * 128],
                    start=(w == 0 and oc == 0), stop=(w == GW - 1 and oc == HC - 1),
                    skip_group_check=True),
                    "s_smm" if (w == GW - 1 and oc == HC - 1) else None, 1)
        smm_at[g] = V("s_smm")
        W("scalar", "s_smm", V("s_smm"))
        W("scalar", "s_ext", V("s_ext"))
        OP("scalar", lambda e: e.activation(sall[:, :, :], pb[4], ACT.Identity), "s_sev", 1)
        sev_at[g] = V("s_sev")

        W("vector", "s_sev", V("s_sev"))
        if g >= 2:
            W("vector", "s_atmm", atmm_at[g - 2])
        for w in range(GW):
            OP("vector", lambda e, w=w: e.max(t32[:, w, 0:8], sall[:, w, :]))
        for w in range(GW):
            OP("vector", lambda e, w=w: e.tensor_copy(scrx[:, w, :], sall[:, w, :]))
        for r in range(1, 4):
            for w in range(GW):
                OP("vector", lambda e, w=w, r=r: e.match_replace(
                    scrx[:, w, :], t32[:, w, 8 * (r - 1):8 * r], scrx[:, w, :], BNEG))
            for w in range(GW):
                OP("vector", lambda e, w=w, r=r: e.max(t32[:, w, 8 * r:8 * r + 8], scrx[:, w, :]))
        vnop(3)
        OP("vector", lambda e: e.tensor_scalar(zsv[:, :, 2], t32[:, :, 0], -1.0, None, op0=ALU.mult), "s_ext", 1)
        W("scalar", "s_ext", V("s_ext"))
        for w in range(GW):
            OP("scalar", lambda e, w=w: e.activation(pall[:, w, :], sall[:, w, :], ACT.Exp,
                                                     bias=zsv[:, w, 2:3]))
            OP("scalar", lambda e, w=w: e.activation(et32[:, w, :], t32[:, w, :], ACT.Exp,
                                                     bias=zsv[:, w, 2:3]), "s_esn", 1)
        W("vector", "s_esn", V("s_esn"))
        OP("vector", lambda e: e.tensor_reduce(zsv[:, :, 0], et32[:, :, :], axis=AX, op=ALU.add))
        vnop()
        OP("vector", lambda e: e.reciprocal(zsv[:, :, 1], zsv[:, :, 0]))
        OP("vector", lambda e: e.tensor_tensor(
            scrx[:, :, :], sall[:, :, :], t32[:, :, 31:32].to_broadcast([128, GW, WIN]), op=ALU.is_ge))
        OP("vector", lambda e: e.tensor_tensor(pall[:, :, :], pall[:, :, :], scrx[:, :, :], op=ALU.mult))
        vnop(2)
        OP("vector", lambda e, g=g: e.tensor_tensor(
            attn16[:, g % 2, :, :], pall[:, :, :], zsv[:, :, 1:2].to_broadcast([128, GW, WIN]),
            op=ALU.mult), "s_attn", 1)
        attn_at[g] = V("s_attn")

    def emit_Vp(g):
        for tl in range(GW):
            for half in range(2):
                bank = 2 + half
                W("tensor", "s_vev", V("s_vev"))
                for hc in range(HC):
                    OP("tensor", lambda e, bank=bank, hc=hc, tl=tl, half=half: e.matmul(
                        pb[bank], xng[:, hc, tl * 128:(tl + 1) * 128],
                        qkvT[:, 16 + hc, half * 512:(half + 1) * 512],
                        start=(hc == 0), stop=False), None, 1)
                OP("tensor", lambda e, bank=bank, half=half: e.matmul(
                    pb[bank], onef[:, :], bv16[0:1, half * 512:(half + 1) * 512],
                    start=False, stop=True), "s_vmm", 1)
                vmm_hist.append(V("s_vmm"))
                W("scalar", "s_vmm", V("s_vmm"))
                if g >= 1 and tl == 0 and half == 0:
                    W("scalar", "s_avmm", av_at[g - 1])
                OP("scalar", lambda e, tl=tl, half=half, bank=bank: e.activation(
                    vg[:, tl, half * 512:(half + 1) * 512], pb[bank], ACT.Identity), "s_vev", 1)

    def emit_B(g):
        W("scalar", "s_attn", attn_at[g])
        if g >= 1:
            W("scalar", "s_avmm", av_at[g - 1])
        OP("scalar", lambda e, g=g: e.dma_start(
            out=attnT.rearrange("p w k -> p w k"), in_=attn16[:, g % 2, :, :],
            transpose=True), "s_atmm", 16)
        W("tensor", "s_atmm", V("s_atmm"))
        W("tensor", "s_vev", V("s_vev"))
        for w in range(GW):
            j = g * GW + w
            if lrl_hist:
                W("tensor", "s_lrl", lrl_hist[-1])
            for half in range(2):
                OP("tensor", lambda e, w=w, half=half: e.matmul(
                    pb[6 + half], attnT[:, w, :], vg[:, w, half * 512:(half + 1) * 512],
                    start=True, stop=True), "s_avmm", 1)
            if j >= 2:
                W("sync", "s_r", r_hist[-2])
            OP("sync", lambda e, j=j: e.dma_start(out=xne[:, j % 2, :],
                                                  in_=xnf_d[j * 128:(j + 1) * 128, :]), "s_xne", 16)
            xne_hist.append(V("s_xne"))
            W("scalar", "s_avmm", V("s_avmm"))
            OP("scalar", lambda e: e.activation(rlsc[:, 0:512], pb[6], ACT.Prelu, alpha=0.01))
            OP("scalar", lambda e: e.activation(rlsc[:, 512:1024], pb[7], ACT.Prelu, alpha=0.01), "s_lrl", 1)
            lrl_hist.append(V("s_lrl"))
            W("vector", "s_lrl", V("s_lrl"))
            W("vector", "s_xne", 16 * (j + 1))
            OP("vector", lambda e, w=w, j=j: e.tensor_tensor(
                rrb[:, w, :], rlsc[:, :], xne[:, j % 2, :], op=ALU.add), "s_r", 1)
            r_hist.append(V("s_r"))
            W("vector", "s_r", V("s_r"))
            OP("vector", lambda e, w=w: e.bn_stats(bnb[:, 0, 0:6], rrb[:, w, 0:512]))
            OP("vector", lambda e, w=w: e.bn_stats(bnb[:, 0, 6:12], rrb[:, w, 512:1024]))
            vnop(4)
            OP("vector", lambda e, w=w: e.bn_aggr(mv2[:, w, :], bnb[:, 0, :]), "s_rh2", 1)
        av_at[g] = V("s_avmm")
        atmm_at[g] = V("s_atmm")

        W("vector", "s_rh2", V("s_rh2"))
        OP("vector", lambda e: e.tensor_scalar(nwt2[:, :, 0], mv2[:, :, 1], EPS, None, op0=ALU.add))
        OP("vector", lambda e: e.tensor_scalar(nmr2[:, :], mv2[:, :, 0], -1.0, None, op0=ALU.mult), "s_rh2", 1)
        W("scalar", "s_rh2", V("s_rh2"))
        OP("scalar", lambda e: e.activation(nwt2[:, :, 1], nwt2[:, :, 0], ACT.Ln))
        snop(2)
        OP("scalar", lambda e: e.activation(rho2[:, :], nwt2[:, :, 1], ACT.Exp, scale=-0.5), "s_sq2", 1)
        W("vector", "s_sq2", V("s_sq2"))
        OP("vector", lambda e: e.tensor_tensor(nmr2[:, :], nmr2[:, :], rho2[:, :], op=ALU.mult), "s_rh2", 1)

        for w in range(GW):
            j = g * GW + w
            W("scalar", "s_rh2", V("s_rh2"))
            W("scalar", "s_st", V("s_st"))
            snop(1)
            OP("scalar", lambda e, w=w: e.activation(
                snb[:, :], rrb[:, w, :], ACT.Identity,
                scale=rho2[:, w:w + 1], bias=nmr2[:, w:w + 1]), "s_sn", 1)
            W("vector", "s_sn", V("s_sn"))
            OP("vector", lambda e: e.tensor_tensor(snb[:, :], snb[:, :], gamr[:], op=ALU.mult))
            vnop(2)
            if j >= 2:
                W("vector", "s_ssp", 16 * (j - 1))
            OP("vector", lambda e, j=j: e.tensor_tensor(
                stile[:, j % 2, :], snb[:, :], betr[:], op=ALU.add), "s_st", 1)
            W("sync", "s_st", V("s_st"))
            OP("sync", lambda e, j=j: e.dma_start(
                out=ssp_d[j * 128:(j + 1) * 128, :], in_=stile[:, j % 2, :]), "s_ssp", 16)

    def emit_att_group(g):
        emit_A(g)
        if g >= 1:
            emit_B(g - 1)
        emit_Vp(g)
        if g == GRP - 1:
            emit_B(g)

    LAG = 4
    for j in range(NT + LAG + 1):
        if j < NT:
            if j >= 6:
                W("sync", "s_xrdv", j - 5)
                W("sync", "s_xn", j - 5)
            OP("sync", lambda e, j=j: e.dma_start(out=xr[:, j % 6, :],
                                                  in_=x_d[j * 128:(j + 1) * 128, :]), "s_x1", 16)
            W("vector", "s_x1", 16 * (j + 1))
            OP("vector", lambda e, j=j: e.bn_stats(bnb[:, j % 2, 0:6], xr[:, j % 6, 0:512]))
            OP("vector", lambda e, j=j: e.bn_stats(bnb[:, j % 2, 6:12], xr[:, j % 6, 512:1024]))
            OP("vector", lambda e, j=j: e.tensor_reduce(rmxc[:, j:j + 1], xr[:, j % 6, :], axis=AX, op=ALU.max))
            OP("vector", lambda e, j=j: e.tensor_reduce(rmnc[:, j:j + 1], xr[:, j % 6, :], axis=AX,
                                                        op=ALU.min), "s_xrdv", 1)
            OP("vector", lambda e, j=j: e.bn_aggr(mv[:, j, :], bnb[:, j % 2, :]))
            if j % 4 == 3:
                b = j // 4
                sl = slice(b * 4, b * 4 + 4)
                vnop(4)
                OP("vector", lambda e, sl=sl: e.tensor_scalar(negmu[:, sl], mv[:, sl, 0], -1.0, None,
                                                              op0=ALU.mult))
                OP("vector", lambda e, sl=sl: e.tensor_scalar(nwt[:, :, 0], mv[:, sl, 1], EPS, None,
                                                              op0=ALU.add), "s_veps", 1)
                W("scalar", "s_veps", V("s_veps"))
                OP("scalar", lambda e: e.activation(nwt[:, :, 1], nwt[:, :, 0], ACT.Ln))
                snop(2)
                OP("scalar", lambda e, sl=sl: e.activation(rho[:, sl], nwt[:, :, 1], ACT.Exp,
                                                           scale=-0.5), "s_rho", 1)
                W("vector", "s_rho", V("s_rho"))
                OP("vector", lambda e, sl=sl: e.tensor_tensor(nmr[:, sl], negmu[:, sl], rho[:, sl],
                                                              op=ALU.mult))
                OP("vector", lambda e, sl=sl: e.tensor_tensor(nwt[:, :, 2], rmxc[:, sl], negmu[:, sl], op=ALU.add))
                OP("vector", lambda e, sl=sl: e.tensor_tensor(nwt[:, :, 3], mv[:, sl, 0], rmnc[:, sl], op=ALU.subtract))
                vnop(2)
                OP("vector", lambda e: e.tensor_tensor(nwt[:, :, 2], nwt[:, :, 2], nwt[:, :, 3], op=ALU.max))
                vnop(2)
                OP("vector", lambda e, sl=sl: e.tensor_tensor(amax[:, sl], nwt[:, :, 2], rho[:, sl],
                                                              op=ALU.mult), "s_rho", 1)
            if j == NT - 1:
                W("vector", "s_rho", 2 * (NT // 4))
                OP("vector", lambda e: e.tensor_reduce(lmax[:, 0:1], amax[:], axis=AX, op=ALU.max), "s_gm1", 1)
                W("sync", "s_gm1", 1)
                OP("sync", lambda e: e.dma_start(out=scal_d[0:1, :], in_=lmax[:, 0:1]), "s_f1", 16)
                W("sync", "s_f1", V("s_f1"))
                OP("sync", lambda e: e.dma_start(out=foldb[:], in_=scal_d[0:1, :]), "s_f1", 16)
                W("vector", "s_f1", V("s_f1"))
                OP("vector", lambda e: e.tensor_reduce(sc1[0:1, 0:1], foldb[:], axis=AX, op=ALU.max))
                vnop()
                OP("vector", lambda e: e.memset(ccb[:], BNEG))
                OP("vector", lambda e: e.tensor_copy(ccb[0:1, 0:1], sc1[0:1, 0:1]), "s_gm1", 1)
                W("gpsimd", "s_gm1", 2)
                OP("gpsimd", lambda e: e.dma_start(out=cc1i, in_=ccb[:]), "s_f1", 16)
                W("gpsimd", "s_f1", V("s_f1"))
                OP("gpsimd", lambda e: e.collective_compute("AllReduce", ALU.max, replica_groups=RGRP,
                                                            ins=[cc1i], outs=[cc1o]), "s_ar1", 1)
        i = j - LAG
        if 0 <= i < NT:
            b = i // 4
            W("scalar", "s_rho", 2 * (b + 1))
            W("scalar", "s_x1", min(16 * (i + 1), V("s_x1")))
            if i >= 4:
                W("scalar", "s_xf", 16 * (i - 3))
            snop(1)
            OP("scalar", lambda e, i=i: e.activation(xnt[:, i % 4, :], xr[:, i % 6, :], ACT.Identity,
                                                     scale=rho[:, i:i + 1], bias=nmr[:, i:i + 1]), "s_xn", 1)
            W("sync", "s_xn", i + 1)
            OP("sync", lambda e, i=i: e.dma_start(
                out=xnf_d[i * 128:(i + 1) * 128, :], in_=xnt[:, i % 4, :]), "s_xf", 16)
        if 0 <= i < NT and i % GW == GW - 1:
            emit_att_group(i // GW)

    W("gpsimd", "s_ar1", 1)
    OP("gpsimd", lambda e: e.dma_start(out=sc1[0:1, 6:7], in_=cc1o[0:1, 0:1]), "s_f1", 16)
    W("vector", "s_f1", V("s_f1"))
    W("vector", "s_sig", 2)
    OP("vector", lambda e: e.tensor_tensor(sc1[0:1, 7:8], sc1[0:1, 6:7], sc1[0:1, 4:5], op=ALU.mult))
    vnop()
    OP("vector", lambda e: e.reciprocal(sc1[0:1, 0:1], sc1[0:1, 7:8]))
    vnop()
    OP("vector", lambda e: e.tensor_scalar(sc1[0:1, 0:1], sc1[0:1, 0:1], 127.0, None, op0=ALU.mult))
    OP("vector", lambda e: e.tensor_scalar(sc1[0:1, 1:2], sc1[0:1, 7:8], 1.0 / 127.0, None, op0=ALU.mult), "s_gm1", 1)
    W("sync", "s_gm1", 3)
    OP("sync", lambda e: e.dma_start(out=scal_d[1:2, 0:4], in_=sc1[0:1, 0:4]), "s_f1", 16)
    W("sync", "s_f1", V("s_f1"))
    OP("sync", lambda e: e.dma_start(out=scb[:, 0:1], in_=scal_d[1:2, 0:1].to_broadcast([128, 1])), "s_f1", 16)
    OP("sync", lambda e: e.dma_start(out=scb[:, 1:2], in_=scal_d[1:2, 1:2].to_broadcast([128, 1])), "s_f1", 16)
    W("vector", "s_f1", V("s_f1"))
    OP("vector", lambda e: e.tensor_scalar(Aqv[:], rho[:], scb[:, 0:1], None, op0=ALU.mult))
    OP("vector", lambda e: e.tensor_scalar(Bqv[:], nmr[:], scb[:, 0:1], None, op0=ALU.mult), "s_vec1", 1)
    vec1_all = V("s_vec1")

    kstripe = [24]
    qt_of_stripe = dict(qstripe_qt)
    gmm_at, umm_at, evg_at, stt_at = {}, {}, {}, {}
    sp_hist, stt_hist = [], []
    qt_at_ic = {}

    def stripe_quant(wsrc, row0, scale_store, dst_fn):
        k = kstripe[0]
        kstripe[0] += 1
        if k == 24:
            W("scalar", "s_smm", smm_at[GRP - 1])
        if k >= 2:
            W("scalar", "s_wq1", k - 1)
        OP("scalar", lambda e, wsrc=wsrc, row0=row0, k=k: e.dma_start(
            out=wstrF[:, k % 2, :], in_=wsrc[row0:row0 + 128, :]), "s_wst", 16)
        W("vector", "s_wst", V("s_wst"))
        OP("vector", lambda e, k=k: e.tensor_reduce(
            wr127[:, k % 2, 0:1], wstrF[:, k % 2, :], axis=AX,
            op=ALU.max, apply_absolute_value=True))
        vnop()
        OP("vector", lambda e, k=k: e.reciprocal(wr127[:, k % 2, 1:2], wr127[:, k % 2, 0:1]))
        vnop()
        OP("vector", lambda e, k=k: e.tensor_scalar(wr127[:, k % 2, 2:3], wr127[:, k % 2, 1:2],
                                                    127.0, None, op0=ALU.mult))
        if scale_store is not None:
            OP("vector", lambda e, k=k, scale_store=scale_store: e.tensor_scalar(
                scale_store, wr127[:, k % 2, 0:1], 1.0 / 127.0, None, op0=ALU.mult))
        vnop()
        OP("vector", lambda e, k=k: e.tensor_scalar(
            wsc[:, :], wstrF[:, k % 2, :], wr127[:, k % 2, 2:3], RC,
            op0=ALU.mult, op1=ALU.add))
        if k >= 26:
            W("vector", "s_qt", qt_of_stripe[k - 2])
        OP("vector", lambda e, k=k: e.tensor_scalar(
            mnat1[:, k % 2, :], wsc[:, :], -RC, None, op0=ALU.add), "s_wq1", 1)
        W("scalar", "s_wq1", V("s_wq1"))
        if k == 24:
            W("scalar", "s_vmm", vmm_hist[-1])
            W("scalar", "s_avmm", av_at[GRP - 1])
        OP("scalar", lambda e, k=k, dst_fn=dst_fn: e.dma_start(
            out=dst_fn(), in_=mnat1[:, k % 2, :].rearrange("p (c k) -> p c k", c=HC),
            transpose=True), "s_qt", 16)
        qt_of_stripe[k] = V("s_qt")

    def emit_stripes(ic):
        if ic >= 4:
            W("scalar", "s_umm", umm_at[(ic - 4, 1, 1, 3)])
        for gu, wsrc in ((0, wg_d), (1, wu_d)):
            stripe_quant(wsrc, ic * 128, srow[:, ic, gu:gu + 1],
                         (lambda ic=ic, gu=gu: lhsr_ap(ic, gu)))
        W("vector", "s_vec1", vec1_all)
        OP("vector", lambda e, ic=ic: e.tensor_scalar(cgu[:, ic:ic + 1, :], srow[:, ic:ic + 1, :],
                                                      scb[:, 1:2], None, op0=ALU.mult))
        qt_at_ic[ic] = V("s_qt")

    emit_stripes(0)
    emit_stripes(1)

    for j in range(NT):
        if j == 0:
            W("sync", "s_ssp", 16 * NT)
        if j == 3:
            emit_stripes(2)
        if j == 7:
            emit_stripes(3)
        if j >= 2:
            W("sync", "s_v2", j - 1)
        OP("sync", lambda e, j=j: e.dma_start(out=x2r[:, j % 2, :], in_=x_d[j * 128:(j + 1) * 128, :]), "s_x2", 16)
        W("scalar", "s_x2", 16 * (j + 1))
        W("scalar", "s_vec1", vec1_all)
        if j >= 2:
            W("scalar", "s_ntm", j - 1)
        snop(1)
        OP("scalar", lambda e, j=j: e.activation(p2sc[:, j % 2, :], x2r[:, j % 2, :], ACT.Identity,
                                                 scale=Aqv[:, j:j + 1], bias=Bqv[:, j:j + 1]), "s_v2", 1)
        W("vector", "s_v2", j + 1)
        if j >= 2:
            W("vector", "s_nfm", 16 * (j - 1))
        OP("vector", lambda e, j=j: e.tensor_scalar(p2sc[:, j % 2, :], p2sc[:, j % 2, :],
                                                    CHI, CLO, op0=ALU.min, op1=ALU.max))
        OP("vector", lambda e, j=j: e.tensor_scalar(xnt2[:, j % 2, :], p2sc[:, j % 2, :],
                                                    RC, -RC, op0=ALU.add, op1=ALU.add), "s_ntm", 1)
        qeng = "sync" if j % 2 == 0 else "scalar"
        W(qeng, "s_ntm", j + 1)
        OP(qeng, lambda e, j=j: e.dma_start(
            out=nf[:, :, j * 128:(j + 1) * 128],
            in_=xnt2[:, j % 2, :].rearrange("p (c k) -> p c k", c=HC), transpose=True), "s_nfm", 16)

    for ic in range(IC):
        W("tensor", "s_qt", qt_at_ic[ic])
        lhs_g = lhsr_ap(ic, 0)
        lhs_u = lhsr_ap(ic, 1)
        for tbh in range(2):
            if ic == 0:
                W("tensor", "s_nfm", 16 * (24 if tbh == 0 else NT))
            for gu in range(2):
                lhs = lhs_g if gu == 0 else lhs_u
                for q in range(4):
                    prevkey = (ic, tbh - 1, gu, q) if tbh == 1 else (ic - 1, 1, gu, q)
                    hist = evg_at if gu == 0 else stt_at
                    if prevkey in hist:
                        W("tensor", "s_evg" if gu == 0 else "s_stt", hist[prevkey])
                for hc in range(HC):
                    for q in range(4):
                        tb = tbh * 4 + q
                        bank = gu * 4 + q
                        OP("tensor", lambda e, bank=bank, lhs=lhs, hc=hc, tb=tb: e.matmul(
                            pb[bank], lhs[:, hc, :], nf[:, hc, tb * 512:(tb + 1) * 512],
                            start=(hc == 0), stop=(hc == HC - 1)),
                            (("s_gmm" if gu == 0 else "s_umm") if (hc == HC - 1 and q == 3) else None), 1)
                for q in range(4):
                    (gmm_at if gu == 0 else umm_at)[(ic, tbh, gu, q)] = V("s_gmm" if gu == 0 else "s_umm")
            for q in range(4):
                tb = tbh * 4 + q
                idx = ic * 8 + tb
                slub = slu if q % 2 == 0 else slu2
                W("scalar", "s_gmm", gmm_at[(ic, tbh, 0, q)])
                if len(stt_hist) >= 2:
                    W("scalar", "s_stt", stt_hist[-2])
                W("scalar", "s_vec1", vec1_all)
                OP("scalar", lambda e, q=q, ic=ic, slub=slub: e.activation(
                    slub[:], pb[q], ACT.Silu, scale=cgu[:, ic, 0:1]), "s_evg", 1)
                evg_at[(ic, tbh, 0, q)] = V("s_evg")
                W("vector", "s_umm", umm_at[(ic, tbh, 1, q)])
                W("vector", "s_evg", V("s_evg"))
                if len(sp_hist) >= 2:
                    W("vector", "s_sp", sp_hist[-2])
                OP("vector", lambda e, q=q, ic=ic, slub=slub, idx=idx: e.scalar_tensor_tensor(
                    trng[:, idx % 2, :], pb[4 + q], cgu[:, ic, 1:2], slub[:],
                    op0=ALU.mult, op1=ALU.mult), "s_stt", 1)
                stt_at[(ic, tbh, 1, q)] = V("s_stt")
                stt_hist.append(V("s_stt"))
                OP("vector", lambda e, idx=idx: e.tensor_reduce(
                    gmc[:, idx:idx + 1], trng[:, idx % 2, :], axis=AX,
                    op=ALU.max, apply_absolute_value=True), "s_gmr", 1)
                W("sync", "s_stt", V("s_stt"))
                OP("sync", lambda e, ic=ic, tb=tb, idx=idx: e.dma_start(
                    out=tsp_d[ic, :, tb * 512:(tb + 1) * 512], in_=trng[:, idx % 2, :]), "s_sp", 16)
                sp_hist.append(V("s_sp"))
        if ic + 4 < IC:
            emit_stripes(ic + 4)

    W("vector", "s_gmr", V("s_gmr"))
    vnop()
    OP("vector", lambda e: e.tensor_reduce(lmax[:, 1:2], gmc[:], axis=AX, op=ALU.max), "s_gm2", 1)
    W("sync", "s_gm2", 1)
    OP("sync", lambda e: e.dma_start(out=scal_d[2:3, :], in_=lmax[:, 1:2]), "s_f2", 16)
    W("sync", "s_f2", V("s_f2"))
    OP("sync", lambda e: e.dma_start(out=foldb[:], in_=scal_d[2:3, :]), "s_f2", 16)
    W("vector", "s_f2", V("s_f2"))
    OP("vector", lambda e: e.tensor_reduce(sc1[0:1, 8:9], foldb[:], axis=AX, op=ALU.max))
    vnop()
    OP("vector", lambda e: e.memset(ccb[:], BNEG))
    OP("vector", lambda e: e.tensor_copy(ccb[0:1, 0:1], sc1[0:1, 8:9]), "s_gm2", 1)
    W("gpsimd", "s_gm2", 2)
    OP("gpsimd", lambda e: e.dma_start(out=cc2i, in_=ccb[:]), "s_f2", 16)
    W("gpsimd", "s_f2", V("s_f2"))
    OP("gpsimd", lambda e: e.collective_compute("AllReduce", ALU.max, replica_groups=RGRP,
                                                ins=[cc2i], outs=[cc2o]), "s_ar2", 1)
    W("scalar", "s_stt", V("s_stt"))
    W("scalar", "s_wq1", V("s_wq1"))
    for hc in range(HC):
        if hc >= 1:
            W("scalar", "s_wdq", hc)
        OP("scalar", lambda e, hc=hc: e.dma_start(out=wdh[:, :], in_=wd_d[hc * 128:(hc + 1) * 128, 0:1408]), "s_wdl", 16)
        OP("scalar", lambda e, hc=hc: e.dma_start(out=wdsc[:, :], in_=wd_d[hc * 128:(hc + 1) * 128, 1408:2816]), "s_wdl", 16)
        W("vector", "s_wdl", V("s_wdl"))
        OP("vector", lambda e: e.tensor_reduce(wr127[:, 0, 0:1], wdh[:, :], axis=AX,
                                               op=ALU.max, apply_absolute_value=True))
        OP("vector", lambda e: e.tensor_reduce(wr127[:, 0, 1:2], wdsc[:, :], axis=AX,
                                               op=ALU.max, apply_absolute_value=True))
        vnop()
        OP("vector", lambda e: e.tensor_tensor(wr127[:, 0, 0:1], wr127[:, 0, 0:1], wr127[:, 0, 1:2], op=ALU.max))
        vnop()
        OP("vector", lambda e: e.reciprocal(wr127[:, 0, 2:3], wr127[:, 0, 0:1]))
        vnop()
        OP("vector", lambda e: e.tensor_scalar(wr127[:, 0, 3:4], wr127[:, 0, 2:3], 127.0, None, op0=ALU.mult))
        OP("vector", lambda e, hc=hc: e.tensor_scalar(srd[:, hc:hc + 1], wr127[:, 0, 0:1],
                                                      1.0 / 127.0, None, op0=ALU.mult))
        vnop()
        OP("vector", lambda e: e.tensor_scalar(wdh[:, :], wdh[:, :], wr127[:, 0, 3:4], RC,
                                               op0=ALU.mult, op1=ALU.add))
        OP("vector", lambda e: e.tensor_scalar(wdsc[:, :], wdsc[:, :], wr127[:, 0, 3:4], RC,
                                               op0=ALU.mult, op1=ALU.add))
        if hc >= 1:
            W("vector", "s_wdt", 16 * hc)
        OP("vector", lambda e: e.tensor_scalar(md16[:, 0:1408], wdh[:, :], -RC, None, op0=ALU.add))
        OP("vector", lambda e: e.tensor_scalar(md16[:, 1408:2816], wdsc[:, :], -RC, None, op0=ALU.add), "s_wdq", 1)
        W("scalar", "s_wdq", V("s_wdq"))
        if hc == 0:
            W("scalar", "s_umm", V("s_umm"))
            W("scalar", "s_gmm", V("s_gmm"))
        OP("scalar", lambda e, hc=hc: e.dma_start(
            out=mdT[:, hc, :, :], in_=md16[:, :].rearrange("p (i k) -> p i k", i=IC),
            transpose=True), "s_wdt", 16)

    W("gpsimd", "s_ar2", 1)
    OP("gpsimd", lambda e: e.dma_start(out=sc1[0:1, 10:11], in_=cc2o[0:1, 0:1]), "s_f2", 16)
    W("vector", "s_f2", V("s_f2"))
    OP("vector", lambda e: e.tensor_tensor(sc1[0:1, 11:12], sc1[0:1, 10:11], sc1[0:1, 5:6], op=ALU.mult))
    vnop()
    OP("vector", lambda e: e.reciprocal(sc1[0:1, 8:9], sc1[0:1, 11:12]))
    vnop()
    OP("vector", lambda e: e.tensor_scalar(sc1[0:1, 8:9], sc1[0:1, 8:9], 127.0, None, op0=ALU.mult))
    OP("vector", lambda e: e.tensor_scalar(sc1[0:1, 9:10], sc1[0:1, 11:12], 1.0 / 127.0, None, op0=ALU.mult), "s_gm2", 1)
    W("sync", "s_gm2", 3)
    OP("sync", lambda e: e.dma_start(out=scal_d[3:4, 0:4], in_=sc1[0:1, 8:12]), "s_f2", 16)
    W("sync", "s_f2", V("s_f2"))
    OP("sync", lambda e: e.dma_start(out=scb[:, 2:3], in_=scal_d[3:4, 0:1].to_broadcast([128, 1])), "s_f2", 16)
    OP("sync", lambda e: e.dma_start(out=scb[:, 3:4], in_=scal_d[3:4, 1:2].to_broadcast([128, 1])), "s_f2", 16)
    W("vector", "s_f2", V("s_f2"))
    OP("vector", lambda e: e.engine_nop(), "s_vec2", 1)

    if "scal" in dbg:
        W("sync", "s_vec2", 1)
        OP("sync", lambda e: e.dma_start(out=scal_d[4:5, 0:16], in_=sc1[0:1, :]), "s_dbg", 16)
        OP("sync", lambda e: e.dma_start(out=scal_d[5:6, 0:4], in_=scb[0:1, 0:4]), "s_dbg", 16)

    W("vector", "s_vec2", 1)
    OP("vector", lambda e: e.tensor_scalar(cdw[:, :], srd[:, :], scb[:, 3:4], None, op0=ALU.mult), "s_vec2", 1)

    dmm_at, evd_at = {}, {}
    sg_hist, gt_hist = [], []
    hq_at = {}

    def emit_quant(q):
        for sub in range(2):
            if q == 0 and sub == 0:
                W("sync", "s_wdt", 16 * HC)
            W("sync", "s_hq", V("s_hq"))
            OP("sync", lambda e, q=q, sub=sub: e.dma_start(
                out=tld[:, :, :],
                in_=tsp_d[:, :, q * 512 + sub * 256:q * 512 + (sub + 1) * 256].rearrange(
                    "i p k -> p i k")), "s_tld", 16)
            W("vector", "s_tld", V("s_tld"))
            if q >= 2 and sub == 0:
                W("vector", "s_dmm", dmm_at[(q - 2, HC - 1)])
            tflat = tld.rearrange("p i k -> p (i k)")
            hsl = hqv[:, q % 2, :, sub * 256:(sub + 1) * 256]
            OP("vector", lambda e, tflat=tflat: e.tensor_scalar(
                tflat, tflat, scb[:, 2:3], CHI, op0=ALU.mult, op1=ALU.min))
            OP("vector", lambda e, tflat=tflat: e.tensor_scalar(
                tflat, tflat, CLO, RC, op0=ALU.max, op1=ALU.add))
            OP("vector", lambda e, tflat=tflat, hsl=hsl: e.tensor_scalar(
                hsl, tflat, -RC, None, op0=ALU.add), "s_hq", 1)
        hq_at[q] = V("s_hq")

    def emit_mm(q):
        W("tensor", "s_hq", hq_at[q])
        if q == 0:
            W("tensor", "s_wdt", 16 * HC)
        for hc in range(HC):
            bank = hc % 4
            prev = evd_at.get((q, hc - 4)) if hc >= 4 else (evd_at.get((q - 1, hc + 4)) if q >= 1 else None)
            if prev:
                W("tensor", "s_evd", prev)
            for ic in range(IC):
                OP("tensor", lambda e, bank=bank, hc=hc, ic=ic, q=q: e.matmul(
                    pb[bank], mdT[:, hc, ic, :], hqv[:, q % 2, ic, :],
                    start=(ic == 0), stop=(ic == IC - 1)),
                    "s_dmm" if ic == IC - 1 else None, 1)
            dmm_at[(q, hc)] = V("s_dmm")
            W("scalar", "s_dmm", V("s_dmm"))
            W("scalar", "s_vec2", 2)
            if gt_hist:
                W("scalar", "s_gt", gt_hist[-1])
            OP("scalar", lambda e, hc=hc, bank=bank: e.activation(
                gfm[:, hc, :], pb[bank], ACT.Silu, scale=cdw[:, hc:hc + 1]), "s_evd", 1)
            evd_at[(q, hc)] = V("s_evd")
        W("scalar", "s_evd", evd_at[(q, HC - 1)])
        if q >= 2:
            W("scalar", "s_sg", sg_hist[4 * (q - 1) - 1])
        for hc in range(HC):
            OP("scalar", lambda e, hc=hc, q=q: e.dma_start(
                out=gtmq[:, q % 2, :, hc * 128:(hc + 1) * 128],
                in_=gfm[:, hc, :].rearrange("p (w k) -> p w k", w=GW),
                transpose=True), "s_gt", 16)
        gt_hist.append(V("s_gt"))

    def emit_final(q):
        for w in range(GW):
            j = q * GW + w
            if len(sg_hist) >= 2:
                W("sync", "s_sg", sg_hist[-2])
                W("scalar", "s_sg", sg_hist[-2])
            OP("sync", lambda e, j=j: e.dma_start(out=xr4[:, j % 2, :],
                                                  in_=x_d[j * 128:(j + 1) * 128, :]), "s_x4", 16)
            OP("scalar", lambda e, j=j: e.dma_start(out=sld[:, j % 2, :],
                                                  in_=ssp_d[j * 128:(j + 1) * 128, :]), "s_sldm", 16)
            W("vector", "s_x4", 16 * (j + 1))
            W("vector", "s_sldm", 16 * (j + 1))
            W("vector", "s_gt", gt_hist[q])
            if j >= 2:
                W("vector", "s_ost", 16 * (j - 1))
            OP("vector", lambda e, j=j, w=w, q=q: e.tensor_tensor(
                obuf[:, j % 2, :], sld[:, j % 2, :], gtmq[:, q % 2, w, :], op=ALU.mult))
            OP("vector", lambda e, j=j: e.tensor_tensor(
                obuf[:, j % 2, :], xr4[:, j % 2, :], obuf[:, j % 2, :], op=ALU.add), "s_sg", 1)
            sg_hist.append(V("s_sg"))
            W("sync", "s_sg", V("s_sg"))
            OP("sync", lambda e, j=j: e.dma_start(
                out=out_d[j * 128:(j + 1) * 128, :], in_=obuf[:, j % 2, :]), "s_ost", 16)
            if g_d is not None:
                OP("sync", lambda e, j=j, w=w, q=q: e.dma_start(
                    out=g_d[j * 128:(j + 1) * 128, :], in_=gtmq[:, q % 2, w, :]), "s_dbg", 16)

    emit_quant(0)
    emit_quant(1)
    for q in range(8):
        emit_mm(q)
        if q + 2 < 8:
            emit_quant(q + 2)
        emit_final(q)

    W("sync", "s_ost", 16 * NT)
    if dbg:
        W("sync", "s_dbg", V("s_dbg"))

    block = keep(nc.Block())
    E.replay(block)
    for h in reversed(E._ctx + ctx):
        try:
            h.__exit__(None, None, None)
        except Exception:
            pass
    return nc


def kernel(**inputs):
    x = np.ascontiguousarray(np.asarray(inputs["L0_attn_x"], dtype=np.float32))
    xs = x.reshape(8, T, H)
    full = {
        "wq": np.ascontiguousarray(np.asarray(inputs["L0_attn_Wq"], np.float32)),
        "wk": np.ascontiguousarray(np.asarray(inputs["L0_attn_Wk"], np.float32)),
        "wv": np.ascontiguousarray(np.asarray(inputs["L0_attn_Wv"], np.float32)),
        "bq": np.asarray(inputs["L0_attn_bq"], np.float32).reshape(1, H),
        "bk": np.asarray(inputs["L0_attn_bk"], np.float32).reshape(1, H),
        "bv": np.asarray(inputs["L0_attn_bv"], np.float32).reshape(1, H),
        "gam": np.asarray(inputs["L0_attn_gamma"], np.float32).reshape(1, H),
        "bet": np.asarray(inputs["L0_attn_beta"], np.float32).reshape(1, H),
        "clip": np.array([[np.float32(np.asarray(inputs["L0_ffn_clip_x"]).reshape(-1)[0]),
                           np.float32(np.asarray(inputs["L0_ffn_clip_down"]).reshape(-1)[0])]],
                         dtype=np.float32),
        "wg": np.ascontiguousarray(np.asarray(inputs["L0_ffn_w_gate"], np.float32)),
        "wu": np.ascontiguousarray(np.asarray(inputs["L0_ffn_w_up"], np.float32)),
        "wd": np.ascontiguousarray(np.asarray(inputs["L0_ffn_w_down"], np.float32)),
        "ident": np.eye(128, dtype=np.float16),
    }
    nc = build()
    in_maps = [dict(full, x=np.ascontiguousarray(xs[c])) for c in range(8)]
    res = run_bass_kernel_spmd(nc, in_maps, core_ids=list(range(8)))
    out = np.stack([r["out"] for r in res.results], axis=0)
    return out.reshape(16, 2048, 1024)


if __name__ == "__main__":
    build()
    print("built ok")


# version 09 (final)
# speedup vs baseline: 1.0119x; 1.0119x over previous
import numpy as np
import ml_dtypes
import concourse.bass as bass
import concourse.mybir as mybir
from concourse.bass_utils import run_bass_kernel_spmd

F32 = mybir.dt.float32
F16 = mybir.dt.float16
ALU = mybir.AluOpType
ACT = mybir.ActivationFunctionType
AX = mybir.AxisListType.X

T, NT, H, HC, I, IC = 4096, 32, 1024, 8, 2816, 22
WIN = 128
EPS = 1e-5
RC = 12582912.0
CHI, CLO = 127.25, -128.25
BNEG = -1e30
GRP, GW = 8, 4
RGRP = [[0, 1, 2, 3, 4, 5, 6, 7]]


class Emit:
    ENGS = ("sync", "scalar", "vector", "tensor", "gpsimd")

    def __init__(self, nc):
        self.nc = nc
        self.q = {e: [] for e in self.ENGS}
        self.sems = {}
        self.cnt = {}
        self._ctx = []

    def sem(self, name):
        h = self.nc.semaphore(name)
        s = h.__enter__()
        self._ctx.append(h)
        self.sems[name] = s
        self.cnt[name] = 0

    def op(self, eng, fn, inc=None, n=1):
        if inc is None:
            self.q[eng].append(fn)
        else:
            s = self.sems[inc]
            self.cnt[inc] += n
            self.q[eng].append(lambda e, fn=fn, s=s, n=n: fn(e).then_inc(s, n))

    def wait(self, eng, semname, val=None):
        v = self.cnt[semname] if val is None else val
        assert v <= self.cnt[semname], f"wait beyond appended count on {semname}: {v} > {self.cnt[semname]}"
        s = self.sems[semname]
        if v > 0:
            self.q[eng].append(lambda e, s=s, v=v: e.wait_ge(s, v))

    def val(self, semname):
        return self.cnt[semname]

    def replay(self, block):
        q = self.q

        @block.sync
        def _(e):
            for f in q["sync"]:
                f(e)

        @block.scalar
        def _(e):
            for f in q["scalar"]:
                f(e)

        @block.vector
        def _(e):
            for f in q["vector"]:
                f(e)

        @block.tensor
        def _(e):
            for f in q["tensor"]:
                f(e)

        @block.gpsimd
        def _(e):
            for f in q["gpsimd"]:
                f(e)


def build(dbg=()):
    dbg = set(dbg)
    nc = bass.Bass("TRN2", target_bir_lowering=False, debug=False)
    ctx = []

    def keep(h):
        t = h.__enter__()
        ctx.append(h)
        return t

    x_d = nc.dram_tensor("x", [T, H], F32, kind="ExternalInput").ap()
    wq_d = nc.dram_tensor("wq", [H, H], F32, kind="ExternalInput").ap()
    wk_d = nc.dram_tensor("wk", [H, H], F32, kind="ExternalInput").ap()
    wv_d = nc.dram_tensor("wv", [H, H], F32, kind="ExternalInput").ap()
    bq_d = nc.dram_tensor("bq", [1, H], F32, kind="ExternalInput").ap()
    bk_d = nc.dram_tensor("bk", [1, H], F32, kind="ExternalInput").ap()
    bv_d = nc.dram_tensor("bv", [1, H], F32, kind="ExternalInput").ap()
    gam_d = nc.dram_tensor("gam", [1, H], F32, kind="ExternalInput").ap()
    bet_d = nc.dram_tensor("bet", [1, H], F32, kind="ExternalInput").ap()
    clip_d = nc.dram_tensor("clip", [1, 2], F32, kind="ExternalInput").ap()
    wg_d = nc.dram_tensor("wg", [I, H], F32, kind="ExternalInput").ap()
    wu_d = nc.dram_tensor("wu", [I, H], F32, kind="ExternalInput").ap()
    wd_d = nc.dram_tensor("wd", [H, I], F32, kind="ExternalInput").ap()
    id_d = nc.dram_tensor("ident", [128, 128], F16, kind="ExternalInput").ap()
    out_d = nc.dram_tensor("out", [T, H], F32, kind="ExternalOutput").ap()

    def scr(name, shape, dt):
        kind = "ExternalOutput" if name in dbg else "Internal"
        return nc.dram_tensor(name, shape, dt, kind=kind).ap()

    xnf_d = scr("xnf", [T, H], F16)
    g_d = scr("gdbg", [T, H], F16) if "gdbg" in dbg else None
    tsp_d = scr("tsp", [IC, 128, T], F32)
    ssp_d = scr("ssp", [T, H], F16)
    scal_d = scr("scal", [6, 128], F32)
    cc1i = nc.dram_tensor("cc1i", [1, 16], F32, kind="Internal").ap()
    cc1o = nc.dram_tensor("cc1o", [1, 16], F32, kind="Internal").ap()
    cc2i = nc.dram_tensor("cc2i", [1, 16], F32, kind="Internal").ap()
    cc2o = nc.dram_tensor("cc2o", [1, 16], F32, kind="Internal").ap()

    sb = lambda n, s, d, **k: keep(nc.sbuf_tensor(n, s, d, **k))

    Bar = sb("arB", [128, 32768], F16, is_transpose_dest=True)
    Car = sb("arC", [128, 32768], F16, is_transpose_dest=True)
    Dar = sb("arD", [128, 30720], F16, is_transpose_dest=True)
    Bf, Cf, Df = Bar.ap(), Car.ap(), Dar.ap()
    Bf32, Cf32, Df32 = Bf.bitcast(F32), Cf.bitcast(F32), Df.bitcast(F32)

    xr = Bf32[:, 0:6144].rearrange("p (r k) -> p r k", r=6)
    xnt = Bf[:, 12288:16384].rearrange("p (r k) -> p r k", r=4)
    nf = Bf.rearrange("p (c t) -> p c t", c=HC)
    mdT = Bf[:, 0:22528].rearrange("p (c i k) -> p c i k", c=HC, i=IC)

    qkvT = Cf[:, 0:24576].rearrange("p (m o) -> p m o", m=24)
    xng = Cf[:, 24576:28672].rearrange("p (c t) -> p c t", c=HC)
    vg = Cf[:, 28672:32768].rearrange("p (w k) -> p w k", w=GW)

    def lhsr_ap(ic, gu):
        base = 24576 + (ic % 4) * 2048 + gu * 1024
        return Cf[:, base:base + 1024].rearrange("p (c k) -> p c k", c=HC)

    hqv = Cf[:, 0:22528].rearrange("p (r i k) -> p r i k", r=2, i=IC)

    wstr = Df32[:, 0:2048].rearrange("p (r k) -> p r k", r=2)
    x3r = Df32[:, 0:2048].rearrange("p (r k) -> p r k", r=2)
    x2r = Df32[:, 0:2048].rearrange("p (r k) -> p r k", r=2)
    mnat = Df[:, 4096:6144].rearrange("p (r k) -> p r k", r=2)
    qg = Bf[:, 16384:24576].rearrange("p (r c t) -> p r c t", r=2, c=8)
    kg = Bf[:, 24576:32768].rearrange("p (r c t) -> p r c t", r=2, c=8)
    sall = Df32[:, 3072:3584].rearrange("p (w k) -> p w k", w=GW)
    pall = Df32[:, 3584:4096].rearrange("p (w k) -> p w k", w=GW)
    scrx = Df32[:, 4096:4608].rearrange("p (w k) -> p w k", w=GW)
    t32 = Df32[:, 4608:4736].rearrange("p (w k) -> p w k", w=GW)
    et32 = Df32[:, 4736:4864].rearrange("p (w k) -> p w k", w=GW)
    zsv = Df32[:, 4864:4880].rearrange("p (w k) -> p w k", w=GW)
    attn16 = Df[:, 9760:10784].rearrange("p (r w k) -> p r w k", r=2, w=GW)
    attnT = Df[:, 10784:11296].rearrange("p (w k) -> p w k", w=GW)
    xne = Df[:, 18976:21024].rearrange("p (r k) -> p r k", r=2)
    rrb = Df[:, 21024:25120].rearrange("p (r k) -> p r k", r=4)
    snb = Df[:, 25120:26144]
    stile = Df[:, 26144:28192].rearrange("p (r k) -> p r k", r=2)
    rlsc = Df32[:, 14096:15120]
    p2sc = Df32[:, 2048:4096].rearrange("p (r k) -> p r k", r=2)
    xnt2 = Df[:, 8192:10240].rearrange("p (r k) -> p r k", r=2)
    wstrF = Cf32[:, 0:2048].rearrange("p (r k) -> p r k", r=2)
    wsc = Df32[:, 7168:8192]
    mnat1 = Df[:, 6144:8192].rearrange("p (r k) -> p r k", r=2)
    slu = Df32[:, 6144:6656]
    slu2 = Df32[:, 6656:7168]
    trng = Df32[:, 5120:6144].rearrange("p (r k) -> p r k", r=2)
    wdh = Df32[:, 0:1408]
    wdsc = Df32[:, 1408:2816]
    md16 = Df[:, 5632:5632 + I]
    tld = Df32[:, 0:5632].rearrange("p (i k) -> p i k", i=IC)
    gfm = Df[:, 11264:15360].rearrange("p (c k) -> p c k", c=HC)
    gtmq = Df[:, 15360:23552].rearrange("p (r w k) -> p r w k", r=2, w=GW)
    xr4 = Df32[:, 11776:13824].rearrange("p (r k) -> p r k", r=2)
    sld = Bf[:, 26624:28672].rearrange("p (r k) -> p r k", r=2)
    obuf = Bf32[:, 11264:13312].rearrange("p (r k) -> p r k", r=2)

    mv = sb("mv", [128, NT, 2], F32)
    rho = sb("rho", [128, NT], F32)
    negmu = sb("negmu", [128, NT], F32)
    nmr = sb("nmr", [128, NT], F32)
    amax = sb("amax", [128, NT], F32)
    rmxc = sb("rmxc", [128, NT], F32)
    rmnc = sb("rmnc", [128, NT], F32)
    Aqv = sb("Aqv", [128, NT], F32)
    Bqv = sb("Bqv", [128, NT], F32)
    nwt = sb("nwt", [128, 4, 4], F32)
    biasq = sb("biasq", [128, 8, 3], F32)
    gamr = sb("gamr", [128, H], F16)
    betr = sb("betr", [128, H], F16)
    srow = sb("srow", [128, IC, 2], F32)
    cgu = sb("cgu", [128, IC, 2], F32)
    srd = sb("srd", [128, HC], F32)
    cdw = sb("cdw", [128, HC], F32)
    gmc = sb("gmc", [128, IC * 8], F32)
    wr127 = sb("wr127", [128, 2, 4], F32)
    sc1 = sb("sc1", [1, 16], F32)
    scb = sb("scb", [128, 4], F32)
    foldb = sb("foldb", [1, 128], F32)
    lmax = sb("lmax", [128, 2], F32)
    ccb = sb("ccb", [1, 16], F32)
    clipsb = sb("clipsb", [1, 2], F32)
    c32 = sb("c32", [128, 1], F32)
    onef = sb("onef", [1, 128], F16)
    idsb = sb("idsb", [128, 128], F16)
    bv16 = sb("bv16", [1, H], F16)
    bvf = sb("bvf", [1, H], F32)
    mv2 = sb("mv2", [128, GW, 2], F32)
    bnb = sb("bnb", [128, 2, 12], F32)
    rho2 = sb("rho2", [128, GW], F32)
    nmr2 = sb("nmr2", [128, GW], F32)
    nwt2 = sb("nwt2", [128, GW, 2], F32)
    jnk = sb("jnk", [128, 64], F32)

    psb = [keep(nc.psum_tensor(f"pbank{i}", [128, 512], F32)) for i in range(8)]
    pb = [p.ap() for p in psb]
    pb16 = [p.bitcast(F16) for p in pb]

    E = Emit(nc)
    for s in ("x1 xrdv veps rho xn xf "
              "qld qt sig "
              "gm1 f1 ar1 vec1 "
              "x2 v2 ntm nfm "
              "wst wsg wq1 "
              "gmm evg umm stt sp gmr gm2 f2 ar2 vec2 "
              "xgl qmm qev vmm vev smm sev ext esn attn atmm atv avmm "
              "x3 xne r lrl rh2 sq2 sn st ssp "
              "wdl wdq wdt "
              "tld hq dmm evd gt x4 sldm sg ost dbg").split():
        E.sem("s_" + s)
    OP, W, V = E.op, E.wait, E.val

    def snop(n=2):
        for _ in range(n):
            OP("scalar", lambda e: e.activation(jnk[:, 0:64], jnk[:, 0:64], ACT.Identity))

    def vnop(n=3):
        for _ in range(n):
            OP("vector", lambda e: e.engine_nop())

    OP("scalar", lambda e: e.dma_start(out=clipsb[:], in_=clip_d), "s_qld", 16)
    OP("scalar", lambda e: e.dma_start(
        out=biasq[:, :, 0], in_=bq_d.rearrange("z (c p) -> p (z c)", p=128),
        allow_slow_non_contiguous=True), "s_qld", 16)
    OP("scalar", lambda e: e.dma_start(
        out=biasq[:, :, 1], in_=bk_d.rearrange("z (c p) -> p (z c)", p=128),
        allow_slow_non_contiguous=True), "s_qld", 16)
    OP("scalar", lambda e: e.dma_start(out=idsb[:], in_=id_d), "s_qld", 16)
    OP("scalar", lambda e: e.dma_start(out=bvf[:], in_=bv_d), "s_qld", 16)
    OP("gpsimd", lambda e: e.dma_start(out=gamr[:], in_=gam_d.to_broadcast([128, H])), "s_qld", 16)
    OP("gpsimd", lambda e: e.dma_start(out=betr[:], in_=bet_d.to_broadcast([128, H])), "s_qld", 16)
    qld_all = V("s_qld")
    W("vector", "s_qld", qld_all)
    OP("vector", lambda e: e.memset(jnk[:], 1.0))
    OP("vector", lambda e: e.memset(c32[:], 1.0 / 32.0))
    OP("vector", lambda e: e.memset(onef[:], 1.0))
    OP("vector", lambda e: e.tensor_copy(bv16[:], bvf[:]))
    OP("vector", lambda e: e.tensor_scalar(biasq[:, :, 2], biasq[:, :, 0], 1.0 / 32.0, None, op0=ALU.mult), "s_vec1", 1)
    W("scalar", "s_qld", qld_all)
    snop(1)
    OP("scalar", lambda e: e.activation(sc1[0:1, 2:4], clipsb[0:1, 0:2], ACT.Exp, scale=-1.0), "s_sig", 1)
    W("vector", "s_sig", 1)
    OP("vector", lambda e: e.tensor_scalar(sc1[0:1, 2:4], sc1[0:1, 2:4], 1.0, None, op0=ALU.add))
    vnop()
    OP("vector", lambda e: e.reciprocal(sc1[0:1, 4:6], sc1[0:1, 2:4]), "s_sig", 1)

    qstripe_qt = {}
    for k24 in range(24):
        mi, st8 = divmod(k24, 8)
        wsrc = (wq_d, wk_d, wv_d)[mi]
        ldq = "scalar" if k24 % 2 == 0 else "gpsimd"
        ldsem = "s_wst" if k24 % 2 == 0 else "s_wsg"
        if k24 >= 2:
            W(ldq, "s_wq1", k24 - 1)
        OP(ldq, lambda e, wsrc=wsrc, st8=st8, k24=k24: e.dma_start(
            out=wstr[:, k24 % 2, :], in_=wsrc[st8 * 128:(st8 + 1) * 128, :]), ldsem, 16)
        W("vector", ldsem, V(ldsem))
        if k24 >= 2:
            W("vector", "s_qt", qstripe_qt[k24 - 2])
        OP("vector", lambda e, k24=k24: e.tensor_copy(mnat[:, k24 % 2, :], wstr[:, k24 % 2, :]), "s_wq1", 1)
        W("scalar", "s_wq1", V("s_wq1"))
        OP("scalar", lambda e, mi=mi, st8=st8, k24=k24: e.dma_start(
            out=qkvT[:, mi * 8:(mi + 1) * 8, st8 * 128:(st8 + 1) * 128],
            in_=mnat[:, k24 % 2, :].rearrange("p (c k) -> p c k", c=HC),
            transpose=True), "s_qt", 16)
        qstripe_qt[k24] = V("s_qt")
    qkvT_done = V("s_qt")

    vmm_hist, lrl_hist, r_hist, xne_hist, qev_hist = [], [], [], [], []
    smm_at, attn_at, av_at, sev_at, atmm_at = {}, {}, {}, {}, {}

    def emit_A(g):
        if g == 0:
            W("tensor", "s_qt", qkvT_done)
        if g >= 1:
            W("sync", "s_vmm", vmm_hist[-1])
        W("sync", "s_xf", 16 * (g * GW + GW))
        for w in range(GW):
            j4 = g * GW + w
            OP("sync", lambda e, j4=j4, w=w: e.dma_start(
                out=xng[:, :, w * 128:(w + 1) * 128],
                in_=xnf_d[j4 * 128:(j4 + 1) * 128, :].rearrange("t (c k) -> t c k", c=HC),
                transpose=True), "s_xgl", 16)
        W("tensor", "s_xgl", V("s_xgl"))

        for qk in range(2):
            for oc in range(HC):
                bank = oc % 2
                if len(qev_hist) >= 2:
                    W("tensor", "s_qev", qev_hist[-2])
                for hc in range(HC):
                    OP("tensor", lambda e, bank=bank, qk=qk, hc=hc, oc=oc, g=g: e.matmul(
                        pb[bank], qkvT[:, qk * 8 + hc, oc * 128:(oc + 1) * 128], xng[:, hc, :],
                        start=(hc == 0), stop=(hc == HC - 1)),
                        "s_qmm" if hc == HC - 1 else None, 1)
                W("scalar", "s_qmm", V("s_qmm"))
                if g >= 2:
                    W("scalar", "s_smm", smm_at[g - 2])
                if qk == 0:
                    OP("scalar", lambda e, oc=oc, bank=bank, g=g: e.activation(
                        qg[:, g % 2, oc, :], pb[bank], ACT.Identity, scale=c32[:, 0:1],
                        bias=biasq[:, oc, 2:3]), "s_qev", 1)
                else:
                    OP("scalar", lambda e, oc=oc, bank=bank, g=g: e.activation(
                        kg[:, g % 2, oc, :], pb[bank], ACT.Identity, bias=biasq[:, oc, 1:2]), "s_qev", 1)
                qev_hist.append(V("s_qev"))

        W("tensor", "s_qev", V("s_qev"))
        W("tensor", "s_sev", V("s_sev"))
        ps_sc = pb[4].rearrange("p (w k) -> p w k", w=GW)
        for w in range(GW):
            for oc in range(HC):
                OP("tensor", lambda e, w=w, oc=oc, ps_sc=ps_sc, g=g: e.matmul(
                    ps_sc[:, w, :], qg[:, g % 2, oc, w * 128:(w + 1) * 128],
                    kg[:, g % 2, oc, w * 128:(w + 1) * 128],
                    start=(w == 0 and oc == 0), stop=(w == GW - 1 and oc == HC - 1),
                    skip_group_check=True),
                    "s_smm" if (w == GW - 1 and oc == HC - 1) else None, 1)
        smm_at[g] = V("s_smm")
        W("scalar", "s_smm", V("s_smm"))
        W("scalar", "s_ext", V("s_ext"))
        OP("scalar", lambda e: e.activation(sall[:, :, :], pb[4], ACT.Identity), "s_sev", 1)
        sev_at[g] = V("s_sev")

        W("vector", "s_sev", V("s_sev"))
        if g >= 2:
            W("vector", "s_atmm", atmm_at[g - 2])
        for w in range(GW):
            OP("vector", lambda e, w=w: e.max(t32[:, w, 0:8], sall[:, w, :]))
        for w in range(GW):
            OP("vector", lambda e, w=w: e.tensor_copy(scrx[:, w, :], sall[:, w, :]))
        for r in range(1, 4):
            for w in range(GW):
                OP("vector", lambda e, w=w, r=r: e.match_replace(
                    scrx[:, w, :], t32[:, w, 8 * (r - 1):8 * r], scrx[:, w, :], BNEG))
            for w in range(GW):
                OP("vector", lambda e, w=w, r=r: e.max(t32[:, w, 8 * r:8 * r + 8], scrx[:, w, :]))
        vnop(3)
        OP("vector", lambda e: e.tensor_scalar(zsv[:, :, 2], t32[:, :, 0], -1.0, None, op0=ALU.mult), "s_ext", 1)
        W("scalar", "s_ext", V("s_ext"))
        for w in range(GW):
            OP("scalar", lambda e, w=w: e.activation(pall[:, w, :], sall[:, w, :], ACT.Exp,
                                                     bias=zsv[:, w, 2:3]))
            OP("scalar", lambda e, w=w: e.activation(et32[:, w, :], t32[:, w, :], ACT.Exp,
                                                     bias=zsv[:, w, 2:3]), "s_esn", 1)
        W("vector", "s_esn", V("s_esn"))
        OP("vector", lambda e: e.tensor_reduce(zsv[:, :, 0], et32[:, :, :], axis=AX, op=ALU.add))
        vnop()
        OP("vector", lambda e: e.reciprocal(zsv[:, :, 1], zsv[:, :, 0]))
        OP("vector", lambda e: e.tensor_tensor(
            scrx[:, :, :], sall[:, :, :], t32[:, :, 31:32].to_broadcast([128, GW, WIN]), op=ALU.is_ge))
        OP("vector", lambda e: e.tensor_tensor(pall[:, :, :], pall[:, :, :], scrx[:, :, :], op=ALU.mult))
        vnop(2)
        OP("vector", lambda e, g=g: e.tensor_tensor(
            attn16[:, g % 2, :, :], pall[:, :, :], zsv[:, :, 1:2].to_broadcast([128, GW, WIN]),
            op=ALU.mult), "s_attn", 1)
        attn_at[g] = V("s_attn")

    def emit_Vp(g):
        for tl in range(GW):
            for half in range(2):
                bank = 2 + half
                W("tensor", "s_vev", V("s_vev"))
                for hc in range(HC):
                    OP("tensor", lambda e, bank=bank, hc=hc, tl=tl, half=half: e.matmul(
                        pb[bank], xng[:, hc, tl * 128:(tl + 1) * 128],
                        qkvT[:, 16 + hc, half * 512:(half + 1) * 512],
                        start=(hc == 0), stop=False), None, 1)
                OP("tensor", lambda e, bank=bank, half=half: e.matmul(
                    pb[bank], onef[:, :], bv16[0:1, half * 512:(half + 1) * 512],
                    start=False, stop=True), "s_vmm", 1)
                vmm_hist.append(V("s_vmm"))
                W("scalar", "s_vmm", V("s_vmm"))
                if g >= 1 and tl == 0 and half == 0:
                    W("scalar", "s_avmm", av_at[g - 1])
                OP("scalar", lambda e, tl=tl, half=half, bank=bank: e.activation(
                    vg[:, tl, half * 512:(half + 1) * 512], pb[bank], ACT.Identity), "s_vev", 1)

    def emit_B(g):
        W("scalar", "s_attn", attn_at[g])
        if g >= 1:
            W("scalar", "s_avmm", av_at[g - 1])
        OP("scalar", lambda e, g=g: e.dma_start(
            out=attnT.rearrange("p w k -> p w k"), in_=attn16[:, g % 2, :, :],
            transpose=True), "s_atmm", 16)
        W("tensor", "s_atmm", V("s_atmm"))
        W("tensor", "s_vev", V("s_vev"))
        for w in range(GW):
            j = g * GW + w
            if lrl_hist:
                W("tensor", "s_lrl", lrl_hist[-1])
            for half in range(2):
                OP("tensor", lambda e, w=w, half=half: e.matmul(
                    pb[6 + half], attnT[:, w, :], vg[:, w, half * 512:(half + 1) * 512],
                    start=True, stop=True), "s_avmm", 1)
            if j >= 2:
                W("sync", "s_r", r_hist[-2])
            OP("sync", lambda e, j=j: e.dma_start(out=xne[:, j % 2, :],
                                                  in_=xnf_d[j * 128:(j + 1) * 128, :]), "s_xne", 16)
            xne_hist.append(V("s_xne"))
            W("scalar", "s_avmm", V("s_avmm"))
            OP("scalar", lambda e: e.activation(rlsc[:, 0:512], pb[6], ACT.Prelu, alpha=0.01))
            OP("scalar", lambda e: e.activation(rlsc[:, 512:1024], pb[7], ACT.Prelu, alpha=0.01), "s_lrl", 1)
            lrl_hist.append(V("s_lrl"))
            W("vector", "s_lrl", V("s_lrl"))
            W("vector", "s_xne", 16 * (j + 1))
            OP("vector", lambda e, w=w, j=j: e.tensor_tensor(
                rrb[:, w, :], rlsc[:, :], xne[:, j % 2, :], op=ALU.add), "s_r", 1)
            r_hist.append(V("s_r"))
            W("vector", "s_r", V("s_r"))
            OP("vector", lambda e, w=w: e.bn_stats(bnb[:, 0, 0:6], rrb[:, w, 0:512]))
            OP("vector", lambda e, w=w: e.bn_stats(bnb[:, 0, 6:12], rrb[:, w, 512:1024]))
            vnop(4)
            OP("vector", lambda e, w=w: e.bn_aggr(mv2[:, w, :], bnb[:, 0, :]), "s_rh2", 1)
        av_at[g] = V("s_avmm")
        atmm_at[g] = V("s_atmm")

        W("vector", "s_rh2", V("s_rh2"))
        OP("vector", lambda e: e.tensor_scalar(nwt2[:, :, 0], mv2[:, :, 1], EPS, None, op0=ALU.add))
        OP("vector", lambda e: e.tensor_scalar(nmr2[:, :], mv2[:, :, 0], -1.0, None, op0=ALU.mult), "s_rh2", 1)
        W("scalar", "s_rh2", V("s_rh2"))
        OP("scalar", lambda e: e.activation(nwt2[:, :, 1], nwt2[:, :, 0], ACT.Ln))
        snop(2)
        OP("scalar", lambda e: e.activation(rho2[:, :], nwt2[:, :, 1], ACT.Exp, scale=-0.5), "s_sq2", 1)
        W("vector", "s_sq2", V("s_sq2"))
        OP("vector", lambda e: e.tensor_tensor(nmr2[:, :], nmr2[:, :], rho2[:, :], op=ALU.mult), "s_rh2", 1)

        for w in range(GW):
            j = g * GW + w
            W("scalar", "s_rh2", V("s_rh2"))
            W("scalar", "s_st", V("s_st"))
            snop(1)
            OP("scalar", lambda e, w=w: e.activation(
                snb[:, :], rrb[:, w, :], ACT.Identity,
                scale=rho2[:, w:w + 1], bias=nmr2[:, w:w + 1]), "s_sn", 1)
            W("vector", "s_sn", V("s_sn"))
            OP("vector", lambda e: e.tensor_tensor(snb[:, :], snb[:, :], gamr[:], op=ALU.mult))
            vnop(2)
            if j >= 2:
                W("vector", "s_ssp", 16 * (j - 1))
            OP("vector", lambda e, j=j: e.tensor_tensor(
                stile[:, j % 2, :], snb[:, :], betr[:], op=ALU.add), "s_st", 1)
            W("sync", "s_st", V("s_st"))
            OP("sync", lambda e, j=j: e.dma_start(
                out=ssp_d[j * 128:(j + 1) * 128, :], in_=stile[:, j % 2, :]), "s_ssp", 16)

    def emit_att_group(g):
        emit_A(g)
        if g >= 1:
            emit_B(g - 1)
        emit_Vp(g)
        if g == GRP - 1:
            emit_B(g)

    LAG = 4
    for j in range(NT + LAG + 1):
        if j < NT:
            if j >= 6:
                W("sync", "s_xrdv", j - 5)
                W("sync", "s_xn", j - 5)
            OP("sync", lambda e, j=j: e.dma_start(out=xr[:, j % 6, :],
                                                  in_=x_d[j * 128:(j + 1) * 128, :]), "s_x1", 16)
            W("vector", "s_x1", 16 * (j + 1))
            OP("vector", lambda e, j=j: e.bn_stats(bnb[:, j % 2, 0:6], xr[:, j % 6, 0:512]))
            OP("vector", lambda e, j=j: e.bn_stats(bnb[:, j % 2, 6:12], xr[:, j % 6, 512:1024]))
            OP("vector", lambda e, j=j: e.tensor_reduce(rmxc[:, j:j + 1], xr[:, j % 6, :], axis=AX, op=ALU.max))
            OP("vector", lambda e, j=j: e.tensor_reduce(rmnc[:, j:j + 1], xr[:, j % 6, :], axis=AX,
                                                        op=ALU.min), "s_xrdv", 1)
            OP("vector", lambda e, j=j: e.bn_aggr(mv[:, j, :], bnb[:, j % 2, :]))
            if j % 4 == 3:
                b = j // 4
                sl = slice(b * 4, b * 4 + 4)
                vnop(4)
                OP("vector", lambda e, sl=sl: e.tensor_scalar(negmu[:, sl], mv[:, sl, 0], -1.0, None,
                                                              op0=ALU.mult))
                OP("vector", lambda e, sl=sl: e.tensor_scalar(nwt[:, :, 0], mv[:, sl, 1], EPS, None,
                                                              op0=ALU.add), "s_veps", 1)
                W("scalar", "s_veps", V("s_veps"))
                OP("scalar", lambda e: e.activation(nwt[:, :, 1], nwt[:, :, 0], ACT.Ln))
                snop(2)
                OP("scalar", lambda e, sl=sl: e.activation(rho[:, sl], nwt[:, :, 1], ACT.Exp,
                                                           scale=-0.5), "s_rho", 1)
                W("vector", "s_rho", V("s_rho"))
                OP("vector", lambda e, sl=sl: e.tensor_tensor(nmr[:, sl], negmu[:, sl], rho[:, sl],
                                                              op=ALU.mult))
                OP("vector", lambda e, sl=sl: e.tensor_tensor(nwt[:, :, 2], rmxc[:, sl], negmu[:, sl], op=ALU.add))
                OP("vector", lambda e, sl=sl: e.tensor_tensor(nwt[:, :, 3], mv[:, sl, 0], rmnc[:, sl], op=ALU.subtract))
                vnop(2)
                OP("vector", lambda e: e.tensor_tensor(nwt[:, :, 2], nwt[:, :, 2], nwt[:, :, 3], op=ALU.max))
                vnop(2)
                OP("vector", lambda e, sl=sl: e.tensor_tensor(amax[:, sl], nwt[:, :, 2], rho[:, sl],
                                                              op=ALU.mult), "s_rho", 1)
            if j == NT - 1:
                W("vector", "s_rho", 2 * (NT // 4))
                OP("vector", lambda e: e.tensor_reduce(lmax[:, 0:1], amax[:], axis=AX, op=ALU.max), "s_gm1", 1)
                W("sync", "s_gm1", 1)
                OP("sync", lambda e: e.dma_start(out=scal_d[0:1, :], in_=lmax[:, 0:1]), "s_f1", 16)
                W("sync", "s_f1", V("s_f1"))
                OP("sync", lambda e: e.dma_start(out=foldb[:], in_=scal_d[0:1, :]), "s_f1", 16)
                W("vector", "s_f1", V("s_f1"))
                OP("vector", lambda e: e.tensor_reduce(sc1[0:1, 0:1], foldb[:], axis=AX, op=ALU.max))
                vnop()
                OP("vector", lambda e: e.memset(ccb[:], BNEG))
                OP("vector", lambda e: e.tensor_copy(ccb[0:1, 0:1], sc1[0:1, 0:1]), "s_gm1", 1)
                W("gpsimd", "s_gm1", 2)
                OP("gpsimd", lambda e: e.dma_start(out=cc1i, in_=ccb[:]), "s_f1", 16)
                W("gpsimd", "s_f1", V("s_f1"))
                OP("gpsimd", lambda e: e.collective_compute("AllReduce", ALU.max, replica_groups=RGRP,
                                                            ins=[cc1i], outs=[cc1o]), "s_ar1", 1)
        i = j - LAG
        if 0 <= i < NT:
            b = i // 4
            W("scalar", "s_rho", 2 * (b + 1))
            W("scalar", "s_x1", min(16 * (i + 1), V("s_x1")))
            if i >= 4:
                W("scalar", "s_xf", 16 * (i - 3))
            snop(1)
            OP("scalar", lambda e, i=i: e.activation(xnt[:, i % 4, :], xr[:, i % 6, :], ACT.Identity,
                                                     scale=rho[:, i:i + 1], bias=nmr[:, i:i + 1]), "s_xn", 1)
            W("sync", "s_xn", i + 1)
            OP("sync", lambda e, i=i: e.dma_start(
                out=xnf_d[i * 128:(i + 1) * 128, :], in_=xnt[:, i % 4, :]), "s_xf", 16)
        if 0 <= i < NT and i % GW == GW - 1:
            emit_att_group(i // GW)

    W("gpsimd", "s_ar1", 1)
    OP("gpsimd", lambda e: e.dma_start(out=sc1[0:1, 6:7], in_=cc1o[0:1, 0:1]), "s_f1", 16)
    W("vector", "s_f1", V("s_f1"))
    W("vector", "s_sig", 2)
    OP("vector", lambda e: e.tensor_tensor(sc1[0:1, 7:8], sc1[0:1, 6:7], sc1[0:1, 4:5], op=ALU.mult))
    vnop()
    OP("vector", lambda e: e.reciprocal(sc1[0:1, 0:1], sc1[0:1, 7:8]))
    vnop()
    OP("vector", lambda e: e.tensor_scalar(sc1[0:1, 0:1], sc1[0:1, 0:1], 127.0, None, op0=ALU.mult))
    OP("vector", lambda e: e.tensor_scalar(sc1[0:1, 1:2], sc1[0:1, 7:8], 1.0 / 127.0, None, op0=ALU.mult), "s_gm1", 1)
    W("sync", "s_gm1", 3)
    OP("sync", lambda e: e.dma_start(out=scal_d[1:2, 0:4], in_=sc1[0:1, 0:4]), "s_f1", 16)
    W("sync", "s_f1", V("s_f1"))
    OP("sync", lambda e: e.dma_start(out=scb[:, 0:1], in_=scal_d[1:2, 0:1].to_broadcast([128, 1])), "s_f1", 16)
    OP("sync", lambda e: e.dma_start(out=scb[:, 1:2], in_=scal_d[1:2, 1:2].to_broadcast([128, 1])), "s_f1", 16)
    W("vector", "s_f1", V("s_f1"))
    OP("vector", lambda e: e.tensor_scalar(Aqv[:], rho[:], scb[:, 0:1], None, op0=ALU.mult))
    OP("vector", lambda e: e.tensor_scalar(Bqv[:], nmr[:], scb[:, 0:1], None, op0=ALU.mult), "s_vec1", 1)
    vec1_all = V("s_vec1")

    kstripe = [24]
    qt_of_stripe = dict(qstripe_qt)
    gmm_at, umm_at, evg_at, stt_at = {}, {}, {}, {}
    sp_hist, stt_hist = [], []
    qt_at_ic = {}

    def stripe_quant(wsrc, row0, scale_store, dst_fn):
        k = kstripe[0]
        kstripe[0] += 1
        if k == 24:
            W("scalar", "s_smm", smm_at[GRP - 1])
        if k >= 2:
            W("scalar", "s_wq1", k - 1)
        OP("scalar", lambda e, wsrc=wsrc, row0=row0, k=k: e.dma_start(
            out=wstrF[:, k % 2, :], in_=wsrc[row0:row0 + 128, :]), "s_wst", 16)
        W("vector", "s_wst", V("s_wst"))
        OP("vector", lambda e, k=k: e.tensor_reduce(
            wr127[:, k % 2, 0:1], wstrF[:, k % 2, :], axis=AX,
            op=ALU.max, apply_absolute_value=True))
        vnop()
        OP("vector", lambda e, k=k: e.reciprocal(wr127[:, k % 2, 1:2], wr127[:, k % 2, 0:1]))
        vnop()
        OP("vector", lambda e, k=k: e.tensor_scalar(wr127[:, k % 2, 2:3], wr127[:, k % 2, 1:2],
                                                    127.0, None, op0=ALU.mult))
        if scale_store is not None:
            OP("vector", lambda e, k=k, scale_store=scale_store: e.tensor_scalar(
                scale_store, wr127[:, k % 2, 0:1], 1.0 / 127.0, None, op0=ALU.mult))
        vnop()
        OP("vector", lambda e, k=k: e.tensor_scalar(
            wsc[:, :], wstrF[:, k % 2, :], wr127[:, k % 2, 2:3], RC,
            op0=ALU.mult, op1=ALU.add))
        if k >= 26:
            W("vector", "s_qt", qt_of_stripe[k - 2])
        OP("vector", lambda e, k=k: e.tensor_scalar(
            mnat1[:, k % 2, :], wsc[:, :], -RC, None, op0=ALU.add), "s_wq1", 1)
        W("scalar", "s_wq1", V("s_wq1"))
        if k == 24:
            W("scalar", "s_vmm", vmm_hist[-1])
            W("scalar", "s_avmm", av_at[GRP - 1])
        OP("scalar", lambda e, k=k, dst_fn=dst_fn: e.dma_start(
            out=dst_fn(), in_=mnat1[:, k % 2, :].rearrange("p (c k) -> p c k", c=HC),
            transpose=True), "s_qt", 16)
        qt_of_stripe[k] = V("s_qt")

    def emit_stripes(ic):
        if ic >= 4:
            W("scalar", "s_umm", umm_at[(ic - 4, 1, 1, 3)])
        for gu, wsrc in ((0, wg_d), (1, wu_d)):
            stripe_quant(wsrc, ic * 128, srow[:, ic, gu:gu + 1],
                         (lambda ic=ic, gu=gu: lhsr_ap(ic, gu)))
        W("vector", "s_vec1", vec1_all)
        OP("vector", lambda e, ic=ic: e.tensor_scalar(cgu[:, ic:ic + 1, :], srow[:, ic:ic + 1, :],
                                                      scb[:, 1:2], None, op0=ALU.mult))
        qt_at_ic[ic] = V("s_qt")

    emit_stripes(0)
    emit_stripes(1)

    for j in range(NT):
        if j == 0:
            W("sync", "s_ssp", 16 * NT)
        if j == 3:
            emit_stripes(2)
        if j == 7:
            emit_stripes(3)
        if j >= 2:
            W("sync", "s_v2", j - 1)
        OP("sync", lambda e, j=j: e.dma_start(out=x2r[:, j % 2, :], in_=x_d[j * 128:(j + 1) * 128, :]), "s_x2", 16)
        W("scalar", "s_x2", 16 * (j + 1))
        W("scalar", "s_vec1", vec1_all)
        if j >= 2:
            W("scalar", "s_ntm", j - 1)
        snop(1)
        OP("scalar", lambda e, j=j: e.activation(p2sc[:, j % 2, :], x2r[:, j % 2, :], ACT.Identity,
                                                 scale=Aqv[:, j:j + 1], bias=Bqv[:, j:j + 1]), "s_v2", 1)
        W("vector", "s_v2", j + 1)
        if j >= 2:
            W("vector", "s_nfm", 16 * (j - 1))
        OP("vector", lambda e, j=j: e.tensor_scalar(p2sc[:, j % 2, :], p2sc[:, j % 2, :],
                                                    CHI, CLO, op0=ALU.min, op1=ALU.max))
        OP("vector", lambda e, j=j: e.tensor_scalar(xnt2[:, j % 2, :], p2sc[:, j % 2, :],
                                                    RC, -RC, op0=ALU.add, op1=ALU.add), "s_ntm", 1)
        qeng = "sync" if j % 2 == 0 else "scalar"
        W(qeng, "s_ntm", j + 1)
        OP(qeng, lambda e, j=j: e.dma_start(
            out=nf[:, :, j * 128:(j + 1) * 128],
            in_=xnt2[:, j % 2, :].rearrange("p (c k) -> p c k", c=HC), transpose=True), "s_nfm", 16)

    for ic in range(IC):
        W("tensor", "s_qt", qt_at_ic[ic])
        lhs_g = lhsr_ap(ic, 0)
        lhs_u = lhsr_ap(ic, 1)
        for tbh in range(2):
            if ic == 0:
                W("tensor", "s_nfm", 16 * (24 if tbh == 0 else NT))
            for gu in range(2):
                lhs = lhs_g if gu == 0 else lhs_u
                for q in range(4):
                    prevkey = (ic, tbh - 1, gu, q) if tbh == 1 else (ic - 1, 1, gu, q)
                    hist = evg_at if gu == 0 else stt_at
                    if prevkey in hist:
                        W("tensor", "s_evg" if gu == 0 else "s_stt", hist[prevkey])
                for hc in range(HC):
                    for q in range(4):
                        tb = tbh * 4 + q
                        bank = gu * 4 + q
                        OP("tensor", lambda e, bank=bank, lhs=lhs, hc=hc, tb=tb: e.matmul(
                            pb[bank], lhs[:, hc, :], nf[:, hc, tb * 512:(tb + 1) * 512],
                            start=(hc == 0), stop=(hc == HC - 1)),
                            (("s_gmm" if gu == 0 else "s_umm") if (hc == HC - 1 and q == 3) else None), 1)
                for q in range(4):
                    (gmm_at if gu == 0 else umm_at)[(ic, tbh, gu, q)] = V("s_gmm" if gu == 0 else "s_umm")
            for q in range(4):
                tb = tbh * 4 + q
                idx = ic * 8 + tb
                slub = slu if q % 2 == 0 else slu2
                W("scalar", "s_gmm", gmm_at[(ic, tbh, 0, q)])
                if len(stt_hist) >= 2:
                    W("scalar", "s_stt", stt_hist[-2])
                W("scalar", "s_vec1", vec1_all)
                OP("scalar", lambda e, q=q, ic=ic, slub=slub: e.activation(
                    slub[:], pb[q], ACT.Silu, scale=cgu[:, ic, 0:1]), "s_evg", 1)
                evg_at[(ic, tbh, 0, q)] = V("s_evg")
                W("vector", "s_umm", umm_at[(ic, tbh, 1, q)])
                W("vector", "s_evg", V("s_evg"))
                if len(sp_hist) >= 2:
                    W("vector", "s_sp", sp_hist[-2])
                OP("vector", lambda e, q=q, ic=ic, slub=slub, idx=idx: e.scalar_tensor_tensor(
                    trng[:, idx % 2, :], pb[4 + q], cgu[:, ic, 1:2], slub[:],
                    op0=ALU.mult, op1=ALU.mult), "s_stt", 1)
                stt_at[(ic, tbh, 1, q)] = V("s_stt")
                stt_hist.append(V("s_stt"))
                OP("vector", lambda e, idx=idx: e.tensor_reduce(
                    gmc[:, idx:idx + 1], trng[:, idx % 2, :], axis=AX,
                    op=ALU.max, apply_absolute_value=True), "s_gmr", 1)
                W("sync", "s_stt", V("s_stt"))
                OP("sync", lambda e, ic=ic, tb=tb, idx=idx: e.dma_start(
                    out=tsp_d[ic, :, tb * 512:(tb + 1) * 512], in_=trng[:, idx % 2, :]), "s_sp", 16)
                sp_hist.append(V("s_sp"))
        if ic + 4 < IC:
            emit_stripes(ic + 4)

    W("vector", "s_gmr", V("s_gmr"))
    vnop()
    OP("vector", lambda e: e.tensor_reduce(lmax[:, 1:2], gmc[:], axis=AX, op=ALU.max), "s_gm2", 1)
    W("sync", "s_gm2", 1)
    OP("sync", lambda e: e.dma_start(out=scal_d[2:3, :], in_=lmax[:, 1:2]), "s_f2", 16)
    W("sync", "s_f2", V("s_f2"))
    OP("sync", lambda e: e.dma_start(out=foldb[:], in_=scal_d[2:3, :]), "s_f2", 16)
    W("vector", "s_f2", V("s_f2"))
    OP("vector", lambda e: e.tensor_reduce(sc1[0:1, 8:9], foldb[:], axis=AX, op=ALU.max))
    vnop()
    OP("vector", lambda e: e.memset(ccb[:], BNEG))
    OP("vector", lambda e: e.tensor_copy(ccb[0:1, 0:1], sc1[0:1, 8:9]), "s_gm2", 1)
    W("gpsimd", "s_gm2", 2)
    OP("gpsimd", lambda e: e.dma_start(out=cc2i, in_=ccb[:]), "s_f2", 16)
    W("gpsimd", "s_f2", V("s_f2"))
    OP("gpsimd", lambda e: e.collective_compute("AllReduce", ALU.max, replica_groups=RGRP,
                                                ins=[cc2i], outs=[cc2o]), "s_ar2", 1)
    W("scalar", "s_stt", V("s_stt"))
    W("scalar", "s_wq1", V("s_wq1"))
    for hc in range(HC):
        if hc >= 1:
            W("scalar", "s_wdq", hc)
        OP("scalar", lambda e, hc=hc: e.dma_start(out=wdh[:, :], in_=wd_d[hc * 128:(hc + 1) * 128, 0:1408]), "s_wdl", 16)
        OP("scalar", lambda e, hc=hc: e.dma_start(out=wdsc[:, :], in_=wd_d[hc * 128:(hc + 1) * 128, 1408:2816]), "s_wdl", 16)
        W("vector", "s_wdl", V("s_wdl"))
        OP("vector", lambda e: e.tensor_reduce(wr127[:, 0, 0:1], wdh[:, :], axis=AX,
                                               op=ALU.max, apply_absolute_value=True))
        OP("vector", lambda e: e.tensor_reduce(wr127[:, 0, 1:2], wdsc[:, :], axis=AX,
                                               op=ALU.max, apply_absolute_value=True))
        vnop()
        OP("vector", lambda e: e.tensor_tensor(wr127[:, 0, 0:1], wr127[:, 0, 0:1], wr127[:, 0, 1:2], op=ALU.max))
        vnop()
        OP("vector", lambda e: e.reciprocal(wr127[:, 0, 2:3], wr127[:, 0, 0:1]))
        vnop()
        OP("vector", lambda e: e.tensor_scalar(wr127[:, 0, 3:4], wr127[:, 0, 2:3], 127.0, None, op0=ALU.mult))
        OP("vector", lambda e, hc=hc: e.tensor_scalar(srd[:, hc:hc + 1], wr127[:, 0, 0:1],
                                                      1.0 / 127.0, None, op0=ALU.mult))
        vnop()
        OP("vector", lambda e: e.tensor_scalar(wdh[:, :], wdh[:, :], wr127[:, 0, 3:4], RC,
                                               op0=ALU.mult, op1=ALU.add))
        OP("vector", lambda e: e.tensor_scalar(wdsc[:, :], wdsc[:, :], wr127[:, 0, 3:4], RC,
                                               op0=ALU.mult, op1=ALU.add))
        if hc >= 1:
            W("vector", "s_wdt", 16 * hc)
        OP("vector", lambda e: e.tensor_scalar(md16[:, 0:1408], wdh[:, :], -RC, None, op0=ALU.add))
        OP("vector", lambda e: e.tensor_scalar(md16[:, 1408:2816], wdsc[:, :], -RC, None, op0=ALU.add), "s_wdq", 1)
        W("scalar", "s_wdq", V("s_wdq"))
        if hc == 0:
            W("scalar", "s_umm", V("s_umm"))
            W("scalar", "s_gmm", V("s_gmm"))
        OP("scalar", lambda e, hc=hc: e.dma_start(
            out=mdT[:, hc, :, :], in_=md16[:, :].rearrange("p (i k) -> p i k", i=IC),
            transpose=True), "s_wdt", 16)

    W("gpsimd", "s_ar2", 1)
    OP("gpsimd", lambda e: e.dma_start(out=sc1[0:1, 10:11], in_=cc2o[0:1, 0:1]), "s_f2", 16)
    W("vector", "s_f2", V("s_f2"))
    OP("vector", lambda e: e.tensor_tensor(sc1[0:1, 11:12], sc1[0:1, 10:11], sc1[0:1, 5:6], op=ALU.mult))
    vnop()
    OP("vector", lambda e: e.reciprocal(sc1[0:1, 8:9], sc1[0:1, 11:12]))
    vnop()
    OP("vector", lambda e: e.tensor_scalar(sc1[0:1, 8:9], sc1[0:1, 8:9], 127.0, None, op0=ALU.mult))
    OP("vector", lambda e: e.tensor_scalar(sc1[0:1, 9:10], sc1[0:1, 11:12], 1.0 / 127.0, None, op0=ALU.mult), "s_gm2", 1)
    W("sync", "s_gm2", 3)
    OP("sync", lambda e: e.dma_start(out=scal_d[3:4, 0:4], in_=sc1[0:1, 8:12]), "s_f2", 16)
    W("sync", "s_f2", V("s_f2"))
    OP("sync", lambda e: e.dma_start(out=scb[:, 2:3], in_=scal_d[3:4, 0:1].to_broadcast([128, 1])), "s_f2", 16)
    OP("sync", lambda e: e.dma_start(out=scb[:, 3:4], in_=scal_d[3:4, 1:2].to_broadcast([128, 1])), "s_f2", 16)
    W("vector", "s_f2", V("s_f2"))
    OP("vector", lambda e: e.engine_nop(), "s_vec2", 1)

    if "scal" in dbg:
        W("sync", "s_vec2", 1)
        OP("sync", lambda e: e.dma_start(out=scal_d[4:5, 0:16], in_=sc1[0:1, :]), "s_dbg", 16)
        OP("sync", lambda e: e.dma_start(out=scal_d[5:6, 0:4], in_=scb[0:1, 0:4]), "s_dbg", 16)

    W("vector", "s_vec2", 1)
    OP("vector", lambda e: e.tensor_scalar(cdw[:, :], srd[:, :], scb[:, 3:4], None, op0=ALU.mult), "s_vec2", 1)

    dmm_at, evd_at = {}, {}
    sg_hist, gt_hist, gt_base = [], [], []
    hq_at = {}

    def emit_quant(q):
        for sub in range(2):
            if q == 0 and sub == 0:
                W("sync", "s_wdt", 16 * HC)
            W("sync", "s_hq", V("s_hq"))
            OP("sync", lambda e, q=q, sub=sub: e.dma_start(
                out=tld[:, :, :],
                in_=tsp_d[:, :, q * 512 + sub * 256:q * 512 + (sub + 1) * 256].rearrange(
                    "i p k -> p i k")), "s_tld", 16)
            W("vector", "s_tld", V("s_tld"))
            if q >= 2 and sub == 0:
                W("vector", "s_dmm", dmm_at[(q - 2, HC - 1)])
            tflat = tld.rearrange("p i k -> p (i k)")
            hsl = hqv[:, q % 2, :, sub * 256:(sub + 1) * 256]
            OP("vector", lambda e, tflat=tflat: e.tensor_scalar(
                tflat, tflat, scb[:, 2:3], CHI, op0=ALU.mult, op1=ALU.min))
            OP("vector", lambda e, tflat=tflat: e.tensor_scalar(
                tflat, tflat, CLO, RC, op0=ALU.max, op1=ALU.add))
            OP("vector", lambda e, tflat=tflat, hsl=hsl: e.tensor_scalar(
                hsl, tflat, -RC, None, op0=ALU.add), "s_hq", 1)
        hq_at[q] = V("s_hq")

    def emit_mm(q):
        W("tensor", "s_hq", hq_at[q])
        for hc in range(HC):
            bank = hc % 4
            if q == 0:
                W("tensor", "s_wdt", 16 * (hc + 1))
            prev = evd_at.get((q, hc - 4)) if hc >= 4 else (evd_at.get((q - 1, hc + 4)) if q >= 1 else None)
            if prev:
                W("tensor", "s_evd", prev)
            for ic in range(IC):
                OP("tensor", lambda e, bank=bank, hc=hc, ic=ic, q=q: e.matmul(
                    pb[bank], mdT[:, hc, ic, :], hqv[:, q % 2, ic, :],
                    start=(ic == 0), stop=(ic == IC - 1)),
                    "s_dmm" if ic == IC - 1 else None, 1)
            dmm_at[(q, hc)] = V("s_dmm")
            W("scalar", "s_dmm", V("s_dmm"))
            W("scalar", "s_vec2", 2)
            if q >= 1:
                W("scalar", "s_gt", gt_base[q - 1] + 16 * (hc + 1))
            OP("scalar", lambda e, hc=hc, bank=bank: e.activation(
                gfm[:, hc, :], pb[bank], ACT.Silu, scale=cdw[:, hc:hc + 1]), "s_evd", 1)
            evd_at[(q, hc)] = V("s_evd")
        W("scalar", "s_evd", evd_at[(q, HC - 1)])
        if q >= 2:
            W("scalar", "s_sg", sg_hist[4 * (q - 1) - 1])
        gt_base.append(V("s_gt"))
        for hc in range(HC):
            OP("scalar", lambda e, hc=hc, q=q: e.dma_start(
                out=gtmq[:, q % 2, :, hc * 128:(hc + 1) * 128],
                in_=gfm[:, hc, :].rearrange("p (w k) -> p w k", w=GW),
                transpose=True), "s_gt", 16)
        gt_hist.append(V("s_gt"))

    def emit_final(q):
        for w in range(GW):
            j = q * GW + w
            if len(sg_hist) >= 2:
                W("sync", "s_sg", sg_hist[-2])
                W("scalar", "s_sg", sg_hist[-2])
            OP("sync", lambda e, j=j: e.dma_start(out=xr4[:, j % 2, :],
                                                  in_=x_d[j * 128:(j + 1) * 128, :]), "s_x4", 16)
            OP("scalar", lambda e, j=j: e.dma_start(out=sld[:, j % 2, :],
                                                  in_=ssp_d[j * 128:(j + 1) * 128, :]), "s_sldm", 16)
            W("vector", "s_x4", 16 * (j + 1))
            W("vector", "s_sldm", 16 * (j + 1))
            W("vector", "s_gt", gt_hist[q])
            if j >= 2:
                W("vector", "s_ost", 16 * (j - 1))
            OP("vector", lambda e, j=j, w=w, q=q: e.tensor_tensor(
                obuf[:, j % 2, :], sld[:, j % 2, :], gtmq[:, q % 2, w, :], op=ALU.mult))
            OP("vector", lambda e, j=j: e.tensor_tensor(
                obuf[:, j % 2, :], xr4[:, j % 2, :], obuf[:, j % 2, :], op=ALU.add), "s_sg", 1)
            sg_hist.append(V("s_sg"))
            W("sync", "s_sg", V("s_sg"))
            OP("sync", lambda e, j=j: e.dma_start(
                out=out_d[j * 128:(j + 1) * 128, :], in_=obuf[:, j % 2, :]), "s_ost", 16)
            if g_d is not None:
                OP("sync", lambda e, j=j, w=w, q=q: e.dma_start(
                    out=g_d[j * 128:(j + 1) * 128, :], in_=gtmq[:, q % 2, w, :]), "s_dbg", 16)

    emit_quant(0)
    emit_quant(1)
    for q in range(8):
        emit_mm(q)
        if q + 2 < 8:
            emit_quant(q + 2)
        emit_final(q)

    W("sync", "s_ost", 16 * NT)
    if dbg:
        W("sync", "s_dbg", V("s_dbg"))

    block = keep(nc.Block())
    E.replay(block)
    for h in reversed(E._ctx + ctx):
        try:
            h.__exit__(None, None, None)
        except Exception:
            pass
    return nc


def kernel(**inputs):
    x = np.ascontiguousarray(np.asarray(inputs["L0_attn_x"], dtype=np.float32))
    xs = x.reshape(8, T, H)
    full = {
        "wq": np.ascontiguousarray(np.asarray(inputs["L0_attn_Wq"], np.float32)),
        "wk": np.ascontiguousarray(np.asarray(inputs["L0_attn_Wk"], np.float32)),
        "wv": np.ascontiguousarray(np.asarray(inputs["L0_attn_Wv"], np.float32)),
        "bq": np.asarray(inputs["L0_attn_bq"], np.float32).reshape(1, H),
        "bk": np.asarray(inputs["L0_attn_bk"], np.float32).reshape(1, H),
        "bv": np.asarray(inputs["L0_attn_bv"], np.float32).reshape(1, H),
        "gam": np.asarray(inputs["L0_attn_gamma"], np.float32).reshape(1, H),
        "bet": np.asarray(inputs["L0_attn_beta"], np.float32).reshape(1, H),
        "clip": np.array([[np.float32(np.asarray(inputs["L0_ffn_clip_x"]).reshape(-1)[0]),
                           np.float32(np.asarray(inputs["L0_ffn_clip_down"]).reshape(-1)[0])]],
                         dtype=np.float32),
        "wg": np.ascontiguousarray(np.asarray(inputs["L0_ffn_w_gate"], np.float32)),
        "wu": np.ascontiguousarray(np.asarray(inputs["L0_ffn_w_up"], np.float32)),
        "wd": np.ascontiguousarray(np.asarray(inputs["L0_ffn_w_down"], np.float32)),
        "ident": np.eye(128, dtype=np.float16),
    }
    nc = build()
    in_maps = [dict(full, x=np.ascontiguousarray(xs[c])) for c in range(8)]
    res = run_bass_kernel_spmd(nc, in_maps, core_ids=list(range(8)))
    out = np.stack([r["out"] for r in res.results], axis=0)
    return out.reshape(16, 2048, 1024)


if __name__ == "__main__":
    build()
    print("built ok")
